# version 01
# baseline (speedup 1.0000x reference)
import numpy as np
import ml_dtypes
import concourse.bass as bass
import concourse.mybir as mybir
from concourse.bass_utils import run_bass_kernel_spmd

F32 = mybir.dt.float32
BF16 = mybir.dt.bfloat16
Act = mybir.ActivationFunctionType
Alu = mybir.AluOpType
BFNP = ml_dtypes.bfloat16

C = 2048
TC = 1024
SCT = 512
L = 128
NCHS = 4
NCH = 8
NFT = 16
H = 32
NCORES = 8
KB = 1024
ENGINES = ("sync", "gpsimd", "tensor", "scalar", "vector")


class Sched:

    NDMA = 16

    def __init__(self, nc):
        self.nc = nc
        self.keys = list(ENGINES)
        for e in ("sync", "gpsimd"):
            self.keys += [f"{e}_dma{i}" for i in range(self.NDMA)]
        self.seq = []
        self.count = {k: 0 for k in self.keys}
        self.sems = {k: nc.alloc_semaphore(f"dc_{k}") for k in self.keys}
        self.dmai = {"sync": 0, "gpsimd": 0}

    def job(self, eng, fn, waits=(), inc=1, dma=False, cc=False):
        if cc:
            key = f"{eng}_cc{len(self.keys)}"
            self.keys.append(key)
            self.count[key] = 0
            self.sems[key] = self.nc.alloc_semaphore(key)
        elif dma:
            key = f"{eng}_dma{self.dmai[eng] % self.NDMA}"
            self.dmai[eng] += 1
        else:
            key = eng
        w = [(k, v) for (k, v) in waits if v > 0]
        self.seq.append((key, fn, w, inc))
        self.count[key] += inc
        return (key, self.count[key])

    def tickets(self):
        return {k: self.count[k] for k in self.keys}

    def barrier(self):
        snap = self.tickets()
        out = {}
        for e in ENGINES:
            def fn(o, sem, inc, _e=e, _v=snap[e]):
                if _v > 0:
                    o.wait_ge(self.sems[_e], _v)
                o.nop().then_inc(sem, inc)
            out[e] = self.job(
                e, fn, waits=[(k, snap[k]) for k in self.keys if k != e], inc=1)
        return out

    def emit(self, block):
        for e in ENGINES:
            keys = tuple([e] + [k for k in self.keys
                                if k.startswith(e + "_dma") or
                                k.startswith(e + "_cc")])

            def body(obj, _keys=keys):
                hw = {k: 0 for k in self.keys}
                for (key, fn, waits, inc) in self.seq:
                    if key not in _keys:
                        continue
                    for (wk, wv) in waits:
                        if wk == _keys[0]:
                            continue
                        if wv > hw[wk]:
                            obj.wait_ge(self.sems[wk], wv)
                            hw[wk] = wv
                    fn(obj, self.sems[key], inc)
            getattr(block, e)(body)


def mm_group(tensor, sem, inc, mms):
    last = None
    for m in mms:
        last = m(tensor)
    last.then_inc(sem, inc)


class P:
    pass


def _consts_np():
    one = np.ones((L, L), np.float32)
    tophalf = np.zeros((L, L), np.float32)
    tophalf[:64, :] = 1.0
    colsel = np.zeros((128, 64), np.float32)
    colselh = np.zeros((128, 64), np.float32)
    nsuf = np.zeros((128, 64), np.float32)
    for j in range(8):
        colsel[:, j * 8 + j] = 1.0
        colselh[:64, j * 8 + j] = 1.0
        nsuf[:, j * 8:j * 8 + j] = -1.0
    bh2 = np.zeros((128, 2), np.float32)
    bh2[:64, 0] = 1.0
    bh2[64:, 1] = 1.0
    bc2 = np.zeros((2, 128), np.float32)
    bc2[0, :64] = 1.0
    bc2[1, 64:] = 1.0
    return dict(
        I128F=np.eye(128, dtype=np.float32),
        I128B=np.eye(128, dtype=np.float32).astype(BFNP),
        MK=(np.triu(one) - tophalf).astype(BFNP),
        MR=(np.triu(one, 1) - tophalf).astype(BFNP),
        MKL=(-np.tril(one, -1)).astype(BFNP),
        MASKU=np.triu(one, 1).astype(BFNP),
        COLSEL=colsel.astype(BFNP), COLSELH=colselh.astype(BFNP),
        NSUF=nsuf.astype(BFNP),
        BH2=bh2.astype(BFNP), BC2=bc2,
        ONES1=np.ones((128, 1), np.float32),
        ONESR=np.ones((1, 128), np.float32),
    )


def v3(ap2d, inner):
    return ap2d.rearrange("p (a b) -> p a b", b=inner)


def build(debug=False):
    nc = bass.Bass("TRN2", target_bir_lowering=False, debug=False,
                   detect_race_conditions=False)
    p = P()
    p.nc = nc
    p.debug = debug
    p.s = Sched(nc)

    def ein(name, shape):
        return nc.dram_tensor(name, list(shape), F32, kind="ExternalInput").ap()

    p.xT = ein("xT", (C, TC + 1))
    p.w_l = {k: ein(f"w{k}_l", (16, 128, 16, 128)) for k in ("k", "r", "g", "o")}
    p.wv_r = ein("wv_r", (C, C))
    p.w1_l = ein("w1_l", (16, 128, 160))
    p.w2_l = ein("w2_l", (5, 32, C))
    p.tdw1_l = ein("tdw1_l", (16, 128, 64))
    p.tdw2_r = ein("tdw2_r", (64, C))
    p.tdb = ein("tdb", (128, C))
    p.maa6 = ein("maa6", (128, NFT, 6))
    p.u_h = ein("u_h", (128, NFT))
    p.lnw_h = ein("lnw_h", (128, NFT))
    p.lnb_h = ein("lnb_h", (128, NFT))
    p.pm = ein("pm", (128, 8))
    p.pmi = ein("pmi", (128, 8))
    p.qgu_f32 = ein("qgu_f32", (C, C))
    p.qpr_f32 = ein("qpr_f32", (4 * C, 256))
    p.out = nc.dram_tensor("out", [C, TC], F32, kind="ExternalOutput").ap()

    p.cn = _consts_np()
    p.cni = {k: nc.inline_tensor(v, f"c_{k}") for k, v in p.cn.items()}

    def dram(name, shape, dt, dbg=False):
        kind = "ExternalOutput" if (dbg and debug) else "Internal"
        return nc.dram_tensor(name, list(shape), dt, kind=kind).ap()

    p.spill_k = dram("spill_k", (128, NCH * C), BF16, dbg=True)
    p.spill_v = dram("spill_v", (128, NCH * C), BF16, dbg=True)
    p.spill_e = dram("spill_e", (128, NCH * C), BF16, dbg=True)
    p.spill_x1 = dram("spill_x1", (128, NFT * TC), F32, dbg=True)
    p.qgu_mine = dram("qgu_mine", (16, 128, 16, 128), BF16)
    p.qpr_mine = dram("qpr_mine", (2, 128, 64, 128), BF16)
    p.qgu_all = dram("qgu_all", (8, 16, 128, 16, 128), BF16)
    p.qpr_all = dram("qpr_all", (8, 2, 128, 64, 128), BF16)
    p.msg_in = dram("msg_in", (128, 1040), F32)
    p.msg_all = dram("msg_all", (8, 128, 1040), F32)
    if debug:
        p.dbg_msg = dram("dbg_msg", (8, 128, 1040), F32, dbg=True)
        p.dbg_msgin = dram("dbg_msgin", (128, 1040), F32, dbg=True)
        p.dbg_qgu = dram("dbg_qgu", (16, 128, 16, 128), F32, dbg=True)
        p.dbg_qpr = dram("dbg_qpr", (2, 128, 64, 128), F32, dbg=True)
        p.dbg_clm = dram("dbg_clm", (128, 384), F32, dbg=True)
        p.dbg_D = dram("dbg_D", (128, 128), F32, dbg=True)
        p.dbg_sloc = dram("dbg_sloc", (128, NCH * 1024), F32, dbg=True)
        p.dbg_sct = dram("dbg_sct", (128, NCH * 4096), F32, dbg=True)
        p.dbg_s0 = dram("dbg_s0", (128, NCH * 2048), F32, dbg=True)
        p.dbg_dtm = dram("dbg_dtm", (128, NCH * 32), F32, dbg=True)
        p.dbg_til = dram("dbg_til", (128, NCH * 4096), F32, dbg=True)
        p.dbg_yraw = dram("dbg_yraw", (128, NCH * C), F32, dbg=True)
        p.dbg_ydv = dram("dbg_ydv", (128, NCH * C), F32, dbg=True)
        p.dbg_gst = dram("dbg_gst", (128, NCH * 256), F32, dbg=True)
        p.dbg_y = dram("dbg_y", (128, NCH * C), F32, dbg=True)
        p.dbg_z = dram("dbg_z", (128, NFT * TC), F32, dbg=True)
        p.dbg_srun = dram("dbg_srun", (128, 8 * 1024), F32, dbg=True)

    sb = nc.alloc_sbuf_tensor
    for k in p.cn:
        setattr(p, k, sb("s" + k, list(p.cn[k].shape),
                         BF16 if p.cn[k].dtype == BFNP else F32).ap())
    p.w1_sb = sb("sw1", [128, 16 * 160], BF16).ap()
    p.tdw1_sb = sb("stdw1", [128, 16 * 64], BF16).ap()
    p.tdw2_sb = sb("stdw2", [64, C], BF16).ap()
    p.tdb_sb = sb("stdb", [128, C], F32).ap()
    p.maa_sb = sb("smaa", [128, NFT * 6], F32).ap()
    p.u_sb = sb("su", [128, NFT], F32).ap()
    p.lnw_sb = sb("slnw", [128, NFT], F32).ap()
    p.lnb_sb = sb("slnb", [128, NFT], F32).ap()
    p.pm_sb = sb("spm", [128, 8], F32).ap()
    p.pmi_sb = sb("spmi", [128, 8], F32).ap()
    p.sloc = sb("ssloc", [128, NCH * 1024], BF16).ap()
    p.cLcM = sb("scLcM", [128, 384], F32).ap()
    p.A_ch = sb("sA_ch", [128, 128], F32).ap()
    p.enm = sb("senm", [128, 128], F32).ap()
    p.srun = sb("ssrun", [128, 1024], F32).ap()
    p.epsb = sb("sepsb", [128, 4], F32).ap()

    rem = nc.sbuf_bytes_remaining - 64
    p.arena0, _ = nc.bump_sbuf(rem)
    p.arena_sz = rem
    p._nm = {}

    def at(name, off_kb, shape, dt):
        n = p._nm.get(name, -1) + 1
        p._nm[name] = n
        nm = name if n == 0 else f"{name}_{n}"
        off = int(off_kb * KB)
        off = (off + 31) // 32 * 32
        nbytes = int(np.prod(shape[1:])) * (2 if dt == BF16 else 4)
        assert off + nbytes <= p.arena_sz, f"{nm}: {off}+{nbytes}>{p.arena_sz}"
        return nc.alloc_sbuf_tensor_at(nm, list(shape), dt,
                                       offset=p.arena0 + off).ap()

    p.xbf = at("xbf", 0, (128, NFT * SCT), BF16)
    p.xx = at("xx", 16, (128, NFT * SCT), BF16)
    p.mixA = at("mixA", 32, (128, NFT * SCT), BF16)
    p.mixB = at("mixB", 48, (128, NFT * SCT), BF16)
    p.ksc = at("ksc", 64, (128, NFT * SCT), BF16)
    p.vsc = at("vsc", 80, (128, NCHS * C), BF16)
    p.zsc = at("zsc", 80, (128, NFT * SCT), BF16)
    p.esc = at("esc", 96, (128, NCHS * C), BF16)
    p.gsc = at("gsc", 96, (128, NFT * SCT), BF16)
    p.kch = at("kch", 0, (128, 2 * C), BF16)
    p.vch = at("vch", 8, (128, 2 * C), BF16)
    p.ech = at("ech", 16, (128, 2 * C), BF16)
    p.cmk = at("cmk", 24, (128, C), F32)
    p.cmr = at("cmr", 32, (128, C), F32)
    p.Fk = at("Fk", 40, (128, C), BF16)
    p.Fr = at("Fr", 44, (128, C), BF16)
    p.ktil = at("ktil", 48, (128, C), BF16)
    p.rtil = at("rtil", 52, (128, C), BF16)
    p.scT = at("scT", 56, (128, H * L), BF16)
    p.ysb = at("ysb", 112, (128, C), F32)
    p.khA = at("khA", 120, (128, C), BF16)
    p.khB = at("khB", 124, (128, C), BF16)
    p.S0A = at("S0A", 120, (128, 1024), BF16)
    p.S0B = at("S0B", 124, (128, 1024), BF16)
    p.fkl = at("fkl", 128, (128, C), BF16)
    p.wvsl = at("wvsl", 128, (128, 2 * 512), BF16)
    p.dtm = at("dtm", 128, (128, 32), F32)
    p.gstat = at("gstat", 128.25, (128, 8 * 32), F32)
    p.rku = at("rku", 132, (128, C), BF16)
    p.w2sl = at("w2sl", 132, (32, 2 * C), BF16)
    p.wsl = at("wsl", 140, (128, 2 * C), BF16)
    p.xst = at("xst", 148, (128, 2 * 513), F32)
    p.mpart = at("mpart", 152.01, (32, 5 * SCT), BF16)
    p.t1 = at("t1", 157.02, (64, SCT), BF16)
    p.tmpA = at("tmpA", 158.03, (128, 2 * SCT), F32)
    p.r1 = at("r1", 0, (128, NFT * SCT), F32)
    p.r1sq = at("r1sq", 112, (128, 2 * 512), F32)
    p.rows = at("rows", 116, (1, 1536), F32)
    p.cb_msgs = at("cb_msgs", 0, (128, 8 * 1040), F32)
    p.cb_ae = at("cb_ae", 34, (128, 16), F32)
    p.cb_sm = at("cb_sm", 35, (128, 1024), F32)
    p.cb_t1 = at("cb_t1", 40, (128, 128), F32)
    p.cb_t2 = at("cb_t2", 40.5, (128, 128), F32)
    p.cb_t3 = at("cb_t3", 41, (128, 16), F32)
    p.msg_sb = at("msg_sb", 42, (128, 1040), F32)

    p.f_h = at("f_h", 0, (128, 64 * SCT), BF16)
    p.f_r2 = at("f_r2", 64, (128, NFT * SCT), F32)
    p.f_prsl = at("f_prsl", 96, (128, 2 * 32 * 128), BF16)
    p.f_x1b = at("f_x1b", 112, (128, NFT * SCT), BF16)
    p.f_gusl = at("f_gusl", 128, (128, 2 * C), BF16)
    p.f_x1t = at("f_x1t", 136, (128, 2 * SCT), F32)
    p.f_sg = at("f_sg", 140, (128, SCT), BF16)
    p.f_h2 = at("f_h2", 141, (128, 2 * SCT), F32)
    p.f_rows = at("f_rows", 145, (1, 3 * SCT), F32)
    p.f_rstdb = at("f_rstdb", 151, (128, SCT), F32)

    p.q_w32 = at("q_w32", 0, (128, 2 * C), F32)
    p.q_wbf = at("q_wbf", 16, (128, 2 * C), BF16)
    p.q_abs = at("q_abs", 24, (128, 2 * C), BF16)
    p.q_s = at("q_s", 32, (2, C), BF16)
    p.q_rs = at("q_rs", 36, (2, C), F32)
    p.q_sbc = at("q_sbc", 44, (128, C), F32)
    p.q_r = at("q_r", 52, (128, C), BF16)
    p.q_a = at("q_a", 56, (128, C), BF16)
    p.q_b = at("q_b", 60, (128, C), BF16)
    p.q_q = at("q_q", 64, (128, C), BF16)
    p.q_wh = at("q_wh", 68, (128, C), BF16)
    p.q_t = at("q_t", 72, (128, C), BF16)
    p.q_out = at("q_out", 76, (128, 2 * C), BF16)
    p.q_sbc2 = at("q_sbc2", 84, (128, C), F32)
    p.q_s32 = at("q_s32", 92, (2, C), F32)

    p.PBg = nc.psum_tensor("PBIG", [128, 4096], F32)
    p.PB = p.PBg.__enter__().ap()
    return p


def pb(p, bank, rows=128, cols=512, off=0, dt=F32):
    v = p.PB[0:rows, bank * 512 + off:bank * 512 + off + cols]
    if dt == BF16:
        v = v.bitcast(BF16)
    return v


def ph_consts(p):
    S = p.s

    def ld_sync(o, sem, inc):
        pairs = [
            (p.I128F, "I128F"), (p.I128B, "I128B"), (p.MK, "MK"), (p.MR, "MR"),
            (p.MKL, "MKL"), (p.MASKU, "MASKU"), (p.COLSEL, "COLSEL"),
            (p.COLSELH, "COLSELH"), (p.NSUF, "NSUF"),
            (p.BH2, "BH2"), (p.BC2, "BC2"),
            (p.ONES1, "ONES1"), (p.ONESR, "ONESR"),
        ]
        for dst, k in pairs:
            o.dma_start(out=dst, in_=p.cni[k].ap()).then_inc(sem, 16)
        for dst, src in [(p.tdb_sb, p.tdb), (v3(p.maa_sb, 6), p.maa6),
                         (p.u_sb, p.u_h), (p.lnw_sb, p.lnw_h),
                         (p.lnb_sb, p.lnb_h), (p.pm_sb, p.pm),
                         (p.pmi_sb, p.pmi)]:
            o.dma_start(out=dst, in_=src).then_inc(sem, 16)
    S.job("sync", ld_sync, inc=16 * 20, dma=True)

    def ld_gp(o, sem, inc):
        for dst, src in [(p.tdw2_sb, p.tdw2_r),
                         (v3(p.w1_sb, 160), p.w1_l.transpose([1, 0, 2])),
                         (v3(p.tdw1_sb, 64), p.tdw1_l.transpose([1, 0, 2]))]:
            o.dma_start(out=dst, in_=src).then_inc(sem, 16)
    S.job("gpsimd", ld_gp, inc=16 * 3, dma=True)

    def ld_ms(o, sem, inc):
        o.memset(p.khA[:, :], 0.0)
        o.memset(p.khB[:, :], 0.0)
        o.memset(p.epsb[:, 0:1], 1e-5 * 64.0)
        o.memset(p.epsb[:, 1:2], 1e-5)
        o.memset(p.epsb[:, 2:3], 1.1920929e-07)
        o.memset(p.srun[:, :], 0.0).then_inc(sem, inc)
    S.job("gpsimd", ld_ms, inc=1)
    S.barrier()


def ph_quant(p):
    S = p.s
    pq_sum = pb(p, 0, rows=2, cols=2048)
    pq_bc = pb(p, 4)
    pq_bc2 = pb(p, 5)

    tiles = [("gu", kt, C) for kt in range(16)] + \
            [("pr", kt, 256) for kt in range(64)]
    t_dma = {}
    t_abs = {}
    t_sum = {}
    t_sc = {}
    t_out = {}
    t_od = {}
    t_bc = {}
    t_sbc = {}

    for i, (kind, kt, w) in enumerate(tiles):
        b = (i % 2) * C
        src = (p.qgu_f32 if kind == "gu" else p.qpr_f32)[kt * 128:(kt + 1) * 128, :]
        nsl = max(1, w // 512)

        def dj(o, sem, inc, src=src, b=b, w=w):
            o.dma_start(out=p.q_w32[:, b:b + w], in_=src).then_inc(sem, inc)
        t_dma[i] = S.job("sync", dj, waits=[t_abs.get(i - 2, ("sync", 0))], inc=16, dma=True)

        def aj(o, sem, inc, b=b, w=w):
            o.copy(out=p.q_wbf[:, b:b + w], in_=p.q_w32[:, b:b + w])
            o.activation(out=p.q_abs[:, b:b + w], in_=p.q_wbf[:, b:b + w],
                         func=Act.Abs).then_inc(sem, inc)
        t_abs[i] = S.job("scalar", aj,
                         waits=[t_dma[i], t_out.get(i - 2, ("sync", 0))], inc=1)

        def sj(o, sem, inc, b=b, w=w, nsl=nsl):
            last = None
            for sl in range(nsl):
                sw = min(w, 512)
                last = o.matmul(pq_sum[:, sl * 512:sl * 512 + sw], p.BH2,
                                p.q_abs[:, b + sl * 512:b + sl * 512 + sw],
                                start=True, stop=True)
            last.then_inc(sem, inc)
        t_sum[i] = S.job("tensor", sj,
                         waits=[t_abs[i], t_sc.get(i - 1, ("sync", 0))], inc=1)

        def scj(o, sem, inc, w=w):
            o.tensor_scalar(p.q_s[:, 0:w], pq_sum[:, 0:w], 1.0 / 64.0, None, Alu.mult)
            o.tensor_scalar_max(p.q_s[:, 0:w], p.q_s[:, 0:w], 1e-8)
            o.tensor_copy(out=p.q_s32[:, 0:w], in_=p.q_s[:, 0:w])
            o.reciprocal(p.q_rs[:, 0:w], p.q_s[:, 0:w]).then_inc(sem, inc)
        t_sc[i] = S.job("vector", scj, waits=[t_sum[i]], inc=1)

        for sl in range(nsl):
            sw = min(w, 512)

            def bj(o, sem, inc, sl=sl, sw=sw):
                o.matmul(pq_bc[:, 0:sw], p.BC2, p.q_rs[:, sl * 512:sl * 512 + sw],
                         start=True, stop=True)
                o.matmul(pq_bc2[:, 0:sw], p.BC2, p.q_s32[:, sl * 512:sl * 512 + sw],
                         start=True, stop=True).then_inc(sem, inc)
            t_bc[(i, sl)] = S.job(
                "tensor", bj,
                waits=[t_sc[i], t_sbc.get((i, sl - 1), t_sbc.get((i - 1, 3),
                       t_sbc.get((i - 1, 0), ("sync", 0))))], inc=1)

            def sbj(o, sem, inc, sl=sl, sw=sw):
                o.tensor_copy(out=p.q_sbc[:, sl * 512:sl * 512 + sw],
                              in_=pq_bc[:, 0:sw])
                o.tensor_copy(out=p.q_sbc2[:, sl * 512:sl * 512 + sw],
                              in_=pq_bc2[:, 0:sw]).then_inc(sem, inc)
            t_sbc[(i, sl)] = S.job("vector", sbj, waits=[t_bc[(i, sl)]], inc=1)

        def ej(o, sem, inc, b=b, w=w):
            o.tensor_mul(p.q_r[:, 0:w], p.q_wbf[:, b:b + w], p.q_sbc[:, 0:w])
            o.tensor_single_scalar(p.q_a[:, 0:w], p.q_r[:, 0:w], 0.5, Alu.is_gt)
            o.tensor_single_scalar(p.q_b[:, 0:w], p.q_r[:, 0:w], -0.5, Alu.is_lt)
            o.tensor_sub(p.q_q[:, 0:w], p.q_a[:, 0:w], p.q_b[:, 0:w])
            o.tensor_mul(p.q_wh[:, 0:w], p.q_q[:, 0:w], p.q_sbc2[:, 0:w])
            o.tensor_sub(p.q_t[:, 0:w], p.q_wh[:, 0:w], p.q_wbf[:, b:b + w])
            o.tensor_add(p.q_out[:, b:b + w], p.q_wbf[:, b:b + w],
                         p.q_t[:, 0:w]).then_inc(sem, inc)
        t_out[i] = S.job("vector", ej,
                         waits=[t_sbc[(i, nsl - 1)],
                                t_od.get(i - 2, ("sync", 0))], inc=1)

        def oj(o, sem, inc, kind=kind, kt=kt, b=b, w=w):
            dst = (p.qgu_mine if kind == "gu" else p.qpr_mine)[:, :, kt, :] \
                .transpose([1, 0, 2])
            o.dma_start(out=dst, in_=v3(p.q_out[:, b:b + w], 128)).then_inc(sem, inc)
        t_od[i] = S.job("gpsimd", oj, waits=[t_out[i]], inc=16, dma=True)

    def agj(o, sem, inc):
        o.collective_compute(
            "AllGather", Alu.bypass, replica_groups=[list(range(NCORES))],
            ins=[p.qgu_mine.opt()], outs=[p.qgu_all.opt()]).then_inc(sem, 1)
        o.collective_compute(
            "AllGather", Alu.bypass, replica_groups=[list(range(NCORES))],
            ins=[p.qpr_mine.opt()], outs=[p.qpr_all.opt()]).then_inc(sem, 1)
    p.t_qag = S.job("gpsimd", agj, waits=[t_od[len(tiles) - 1]], inc=2, cc=True)
    S.barrier()


def st_xload(p, sc):
    S = p.s
    t_x = {}
    t_c = {}
    for ft in range(NFT):
        o0 = (ft % 2) * 513

        def dj(o, sem, inc, ft=ft, o0=o0):
            o.dma_start(out=p.xst[:, o0:o0 + 513],
                        in_=p.xT[ft * 128:(ft + 1) * 128, sc * SCT:sc * SCT + 513]) \
                .then_inc(sem, inc)
        t_x[ft] = S.job("sync", dj, waits=[t_c.get((ft - 2, "v"), ("sync", 0)),
                                           t_c.get((ft - 2, "s"), ("sync", 0))],
                        inc=16, dma=True)

        def vj(o, sem, inc, ft=ft, o0=o0):
            o.tensor_sub(p.xx[:, ft * SCT:(ft + 1) * SCT],
                         p.xst[:, o0:o0 + SCT],
                         p.xst[:, o0 + 1:o0 + 513]).then_inc(sem, inc)
        t_c[(ft, "v")] = S.job("vector", vj, waits=[t_x[ft]], inc=1)

        def sj(o, sem, inc, ft=ft, o0=o0):
            o.copy(out=p.xbf[:, ft * SCT:(ft + 1) * SCT],
                   in_=p.xst[:, o0 + 1:o0 + 513]).then_inc(sem, inc)
        t_c[(ft, "s")] = S.job("scalar", sj, waits=[t_x[ft]], inc=1)
    return (t_c[(NFT - 1, "v")], t_c[(NFT - 1, "s")])


def st_mstage(p, fsel, xdone):
    S = p.s
    pm = {f: pb(p, i, rows=32) for i, f in enumerate(fsel)}
    t_xxx = {}
    for ft in range(NFT):
        def vj(o, sem, inc, ft=ft):
            o.scalar_tensor_tensor(
                p.mixA[:, ft * SCT:(ft + 1) * SCT],
                p.xx[:, ft * SCT:(ft + 1) * SCT],
                p.maa_sb[:, ft * 6:ft * 6 + 1],
                p.xbf[:, ft * SCT:(ft + 1) * SCT],
                Alu.mult, Alu.add).then_inc(sem, inc)
        t_xxx[ft] = S.job("vector", vj, waits=list(xdone), inc=1)

    t_mm = {}
    for f in fsel:
        def tj(o, sem, inc, f=f):
            last = None
            for kt in range(NFT):
                last = o.matmul(pm[f][:, :],
                                p.w1_sb[:, kt * 160 + f * 32:kt * 160 + (f + 1) * 32],
                                p.mixA[:, kt * SCT:(kt + 1) * SCT],
                                start=(kt == 0), stop=(kt == NFT - 1))
            last.then_inc(sem, inc)
        t_mm[f] = S.job("tensor", tj, waits=[t_xxx[NFT - 1]], inc=1)

    t_mt = {}
    for f in fsel:
        def sj(o, sem, inc, f=f):
            o.activation(out=p.mpart[0:32, f * SCT:(f + 1) * SCT],
                         in_=pm[f][:, :], func=Act.Tanh).then_inc(sem, inc)
        t_mt[f] = S.job("scalar", sj, waits=[t_mm[f]], inc=1)
    return t_mt


def st_mix(p, f, dest, t_m, side, extra=None, fence=()):
    S = p.s
    pmx = [pb(p, 3), pb(p, 4)]
    w2s = p.w2sl[0:32, side * C:(side + 1) * C]

    def wj(o, sem, inc):
        o.dma_start(out=w2s, in_=p.w2_l[f]).then_inc(sem, inc)
    t_w2 = S.job("gpsimd", wj, waits=([] if extra is None else [extra]), inc=16, dma=True)

    t_mx = {}
    t_ta = {}
    t_mix = {}
    for mt in range(NFT):
        def tj(o, sem, inc, mt=mt):
            o.matmul(pmx[mt % 2][:, :], w2s[:, mt * 128:(mt + 1) * 128],
                     p.mpart[0:32, f * SCT:(f + 1) * SCT],
                     start=True, stop=True).then_inc(sem, inc)
        t_mx[mt] = S.job("tensor", tj,
                         waits=[t_w2, t_m, t_ta.get(mt - 2, ("sync", 0))] +
                               ([] if extra is None else [extra]) +
                               (list(fence) if mt == 0 else []), inc=1)

        def vj(o, sem, inc, mt=mt):
            o.scalar_tensor_tensor(
                p.tmpA[:, (mt % 2) * SCT:(mt % 2 + 1) * SCT],
                pmx[mt % 2][:, :],
                p.maa_sb[:, mt * 6 + f + 1:mt * 6 + f + 2],
                p.xx[:, mt * SCT:(mt + 1) * SCT],
                Alu.add, Alu.mult).then_inc(sem, inc)
        t_ta[mt] = S.job("vector", vj,
                         waits=[t_mx[mt], t_mix.get(mt - 2, ("sync", 0))], inc=1)

        def gj(o, sem, inc, mt=mt):
            o.tensor_add(dest[:, mt * SCT:(mt + 1) * SCT],
                         p.tmpA[:, (mt % 2) * SCT:(mt % 2 + 1) * SCT],
                         p.xbf[:, mt * SCT:(mt + 1) * SCT]).then_inc(sem, inc)
        t_mix[mt] = S.job("gpsimd", gj, waits=[t_ta[mt]], inc=1)
    return t_mix[NFT - 1], t_ta[NFT - 1]


def st_proj_fm(p, wkey, src, t_src, evac, eng_ev="vector", fence=()):
    S = p.s
    pp = [pb(p, 5), pb(p, 6)]
    t_w = {}
    t_k = {}
    t_e = {}
    for mt in range(NFT):
        def wj(o, sem, inc, mt=mt):
            o.dma_start(out=v3(p.wsl[:, (mt % 2) * C:(mt % 2 + 1) * C], 128),
                        in_=p.w_l[wkey][mt]).then_inc(sem, inc)
        t_w[mt] = S.job("gpsimd", wj, waits=[t_k.get(mt - 2, ("sync", 0))], inc=16, dma=True)

        def tj(o, sem, inc, mt=mt):
            last = None
            for kt in range(NFT):
                last = o.matmul(
                    pp[mt % 2][:, :],
                    p.wsl[:, (mt % 2) * C + kt * 128:(mt % 2) * C + (kt + 1) * 128],
                    src[:, kt * SCT:(kt + 1) * SCT],
                    start=(kt == 0), stop=(kt == NFT - 1))
            last.then_inc(sem, inc)
        t_k[mt] = S.job("tensor", tj,
                        waits=[t_w[mt], t_src, t_e.get(mt - 2, ("sync", 0))] +
                              (list(fence) if mt == 0 else []), inc=1)

        def ej(o, sem, inc, mt=mt):
            evac(o, mt, pp[mt % 2][:, :], sem, inc)
        t_e[mt] = S.job(eng_ev, ej, waits=[t_k[mt]], inc=1)
    return t_e


def st_proj_v(p, src, t_src, fence=()):
    S = p.s
    pv = [pb(p, 0), pb(p, 1), pb(p, 2), pb(p, 5)]
    t_w = {}
    t_k = {}
    t_e = {}
    for n_co in range(4):
        for kt in range(NFT):
            i = n_co * NFT + kt

            def wj(o, sem, inc, n_co=n_co, kt=kt, i=i):
                o.dma_start(out=p.wvsl[:, (i % 2) * 512:(i % 2 + 1) * 512],
                            in_=p.wv_r[kt * 128:(kt + 1) * 128,
                                       n_co * 512:(n_co + 1) * 512]).then_inc(sem, inc)
            t_w[i] = S.job("gpsimd", wj, waits=[t_k.get(i - 2, ("sync", 0))], inc=16, dma=True)

            def tj(o, sem, inc, n_co=n_co, kt=kt, i=i):
                last = None
                for cj in range(NCHS):
                    last = o.matmul(
                        pv[cj][:, :],
                        src[:, kt * SCT + cj * 128:kt * SCT + (cj + 1) * 128],
                        p.wvsl[:, (i % 2) * 512:(i % 2 + 1) * 512],
                        start=(kt == 0), stop=(kt == NFT - 1))
                last.then_inc(sem, inc)
            t_k[i] = S.job("tensor", tj,
                           waits=[t_w[i], t_src,
                                  t_e.get(n_co - 1, ("sync", 0))] +
                                 (list(fence) if i == 0 else []), inc=1)

        def ej(o, sem, inc, n_co=n_co):
            for cj in range(NCHS):
                ii = 1 if cj == NCHS - 1 else 1
                inst = o.tensor_copy(
                    out=p.vsc[:, cj * C + n_co * 512:cj * C + (n_co + 1) * 512],
                    in_=pv[cj][:, :])
                inst.then_inc(sem, 1)
        t_e[n_co] = S.job("vector", ej, waits=[t_k[n_co * NFT + NFT - 1]], inc=4)
    return t_e[3]


def st_wpath(p, t_src, fence=()):
    S = p.s
    pw1 = pb(p, 6, rows=64)
    pwt = [pb(p, 3), pb(p, 4)]

    def t1j(o, sem, inc):
        last = None
        for kt in range(NFT):
            last = o.matmul(pw1[:, :], p.tdw1_sb[:, kt * 64:(kt + 1) * 64],
                            p.mixB[:, kt * SCT:(kt + 1) * SCT],
                            start=(kt == 0), stop=(kt == NFT - 1))
        last.then_inc(sem, inc)
    t_s1 = S.job("tensor", t1j, waits=[t_src] + list(fence), inc=1)

    def s1j(o, sem, inc):
        o.activation(out=p.t1[0:64, :], in_=pw1[:, :],
                     func=Act.Tanh).then_inc(sem, inc)
    t_t1 = S.job("scalar", s1j, waits=[t_s1], inc=1)

    t_m = {}
    t_a = {}
    t_ee = {}
    for i in range(16):
        cj, n_co = i // 4, i % 4

        def tj(o, sem, inc, cj=cj, n_co=n_co, i=i):
            o.matmul(pwt[i % 2][:, :], p.t1[0:64, cj * 128:(cj + 1) * 128],
                     p.tdw2_sb[:, n_co * 512:(n_co + 1) * 512],
                     start=True, stop=True).then_inc(sem, inc)
        t_m[i] = S.job("tensor", tj, waits=[t_t1, t_a.get(i - 2, ("sync", 0))],
                       inc=1)

        def vj(o, sem, inc, i=i, n_co=n_co):
            o.tensor_add(p.tmpA[:, (i % 2) * 512:(i % 2 + 1) * 512],
                         pwt[i % 2][:, :],
                         p.tdb_sb[:, n_co * 512:(n_co + 1) * 512]).then_inc(sem, inc)
        t_a[i] = S.job("vector", vj, waits=[t_m[i], t_ee.get(i - 2, ("sync", 0))],
                       inc=1)

        def sj(o, sem, inc, i=i, cj=cj, n_co=n_co):
            o.activation(
                out=p.esc[:, cj * C + n_co * 512:cj * C + (n_co + 1) * 512],
                in_=p.tmpA[:, (i % 2) * 512:(i % 2 + 1) * 512],
                func=Act.Exp).then_inc(sem, inc)
        t_ee[i] = S.job("scalar", sj, waits=[t_a[i]], inc=1)
    return t_ee[15]


def st_summary(p, sc, pcl, t_e, t_k, t_v):
    S = p.s
    paux = [pb(p, 0), pb(p, 1)]
    ptr = pb(p, 2, dt=BF16)
    pS = pb(p, 5, cols=1024)

    t_sle_prev = ("sync", 0)
    for cj in range(NCHS):
        j = sc * NCHS + cj
        co = cj * C
        t_kl = {}
        t_kle = {}
        for n in range(4):
            def tj(o, sem, inc, n=n, co=co):
                o.matmul(paux[n % 2][:, :], p.MKL,
                         p.esc[:, co + n * 512:co + (n + 1) * 512],
                         start=True, stop=True).then_inc(sem, inc)
            t_kl[n] = S.job("tensor", tj,
                            waits=[t_e, t_kle.get(n - 2, ("sync", 0))], inc=1)

            def sj(o, sem, inc, n=n):
                o.activation(out=p.fkl[:, n * 512:(n + 1) * 512],
                             in_=paux[n % 2][:, :], func=Act.Exp).then_inc(sem, inc)
            t_kle[n] = S.job("scalar", sj, waits=[t_kl[n]], inc=1)

        first = (sc == 0 and cj == 0)
        last = (sc == 1 and cj == 3)

        def cj_(o, sem, inc, j=j, co=co, first=first, last=last):
            for ft in range(NFT):
                o.matmul(pcl[:, ft * 8:(ft + 1) * 8],
                         p.esc[:, co + ft * 128:co + (ft + 1) * 128],
                         p.COLSEL[:, j * 8:(j + 1) * 8],
                         start=(first and ft == 0), stop=False)
            for ft in range(NFT):
                o.matmul(pcl[:, 128 + ft * 8:128 + (ft + 1) * 8],
                         p.esc[:, co + ft * 128:co + (ft + 1) * 128],
                         p.COLSELH[:, j * 8:(j + 1) * 8],
                         start=False, stop=False)
            for ft in range(NFT):
                inst = o.matmul(pcl[:, 256 + ft * 8:256 + (ft + 1) * 8],
                                p.esc[:, co + ft * 128:co + (ft + 1) * 128],
                                p.NSUF[:, j * 8:(j + 1) * 8],
                                start=False, stop=(last and ft == NFT - 1))
            inst.then_inc(sem, inc)
        t_cl = S.job("tensor", cj_, waits=[t_e], inc=1)
        if last:
            p.t_clast = t_cl

        t_tr = {}
        t_kh = {}
        for g in range(2):
            def trj(o, sem, inc, g=g, cj=cj):
                last_i = None
                for fl in range(8):
                    ft = g * 8 + fl
                    last_i = o.transpose(
                        ptr[:, fl * 128:(fl + 1) * 128],
                        p.ksc[:, ft * SCT + cj * 128:ft * SCT + (cj + 1) * 128],
                        p.I128B)
                last_i.then_inc(sem, inc)
            t_tr[g] = S.job("tensor", trj,
                            waits=[t_k, t_kh.get(g - 1, ("sync", 0))], inc=1)

            def khj(o, sem, inc, g=g):
                for half, buf in ((0, p.khA), (1, p.khB)):
                    inst = o.tensor_mul(
                        v3(buf, 128)[:, g * 8:(g + 1) * 8, half * 64:half * 64 + 64],
                        v3(ptr, 128)[:, 0:8, half * 64:half * 64 + 64],
                        v3(p.fkl, 128)[:, g * 8:(g + 1) * 8,
                                       half * 64:half * 64 + 64])
                inst.then_inc(sem, inc)
            t_kh[g] = S.job("vector", khj, waits=[t_tr[g], t_kle[3]], inc=1)

        def slj(o, sem, inc, co=co):
            for ft in range(NFT):
                o.matmul(pS[:, ft * 64:(ft + 1) * 64],
                         p.khA[:, ft * 128:(ft + 1) * 128],
                         p.vsc[:, co + ft * 128:co + ft * 128 + 64],
                         start=(ft % 8 == 0), stop=False)
                inst = o.matmul(pS[:, ft * 64:(ft + 1) * 64],
                                p.khB[:, ft * 128:(ft + 1) * 128],
                                p.vsc[:, co + ft * 128 + 64:co + (ft + 1) * 128],
                                start=False, stop=(ft % 8 == 7))
            inst.then_inc(sem, inc)
        t_sl = S.job("tensor", slj, waits=[t_kh[1], t_v, t_sle_prev], inc=1)

        def slev(o, sem, inc, j=j):
            o.tensor_copy(out=p.sloc[:, j * 1024:(j + 1) * 1024],
                          in_=pS[:, :]).then_inc(sem, inc)
        t_sle_prev = S.job("vector", slev, waits=[t_sl], inc=1)

    def spj(o, sem, inc):
        for cj in range(NCHS):
            j = sc * NCHS + cj
            o.dma_start(out=v3(p.spill_k[:, j * C:(j + 1) * C], 128),
                        in_=v3(p.ksc, SCT)[:, :, cj * 128:(cj + 1) * 128]) \
                .then_inc(sem, 16)
            o.dma_start(out=p.spill_v[:, j * C:(j + 1) * C],
                        in_=p.vsc[:, cj * C:(cj + 1) * C]).then_inc(sem, 16)
            o.dma_start(out=p.spill_e[:, j * C:(j + 1) * C],
                        in_=p.esc[:, cj * C:(cj + 1) * C]).then_inc(sem, 16)
    t_sp = S.job("sync", spj, waits=[t_e, t_k, t_v], inc=16 * 12, dma=True)
    return t_sle_prev, t_sp


def ph_pass1_sc(p, sc, pcl):
    S = p.s
    xdone = st_xload(p, sc)
    t_m = st_mstage(p, [0, 1, 2], xdone)
    t_mixk, t_ta_k = st_mix(p, 1, p.mixB, t_m[1], 0)

    def evk(o, mt, ps, sem, inc):
        o.tensor_copy(out=p.ksc[:, mt * SCT:(mt + 1) * SCT],
                      in_=ps).then_inc(sem, inc)
    t_ke = st_proj_fm(p, "k", p.mixB, t_mixk, evk)
    t_mixv, t_ta_v = st_mix(p, 2, p.mixA, t_m[2], 1, fence=[t_ta_k])
    t_ve = st_proj_v(p, p.mixA, t_mixv,
                     fence=[t_m[0], t_m[1], t_m[2], t_ke[NFT - 1]])
    t_mixw, t_ta_w = st_mix(p, 0, p.mixB, t_m[0], 0, extra=t_ke[NFT - 1],
                            fence=[t_ta_v])
    t_ee = st_wpath(p, t_mixw, fence=[t_ke[NFT - 1]])
    t_sle, t_sp = st_summary(p, sc, pcl, t_ee, t_ke[NFT - 1], t_ve)
    return t_sle, t_sp


def ph_msg(p):
    S = p.s

    def evj(o, sem, inc):
        o.tensor_copy(out=p.cLcM[:, :], in_=p.pcl_ap[:, :]).then_inc(sem, inc)
    t_cle = S.job("vector", evj, waits=[p.t_clast], inc=1)

    def prepj(o, sem, inc):
        o.reduce_sum(p.cb_t3[:, :], v3(p.cLcM[:, 0:128], 8),
                     axis=mybir.AxisListType.X).then_inc(sem, inc)
    t_prep = S.job("vector", prepj, waits=[t_cle], inc=1)

    def expj(o, sem, inc):
        o.activation(out=p.cb_t1[:, :], in_=p.cLcM[:, 256:384], func=Act.Exp)
        o.activation(out=p.msg_sb[:, 0:16], in_=p.cb_t3[:, :],
                     func=Act.Exp, scale=-1.0)
        o.activation(out=p.A_ch[:, :], in_=p.cLcM[:, 0:128],
                     func=Act.Exp, scale=-1.0)
        o.activation(out=p.enm[:, :], in_=p.cLcM[:, 128:256],
                     func=Act.Exp, scale=-1.0).then_inc(sem, inc)
    t_exp = S.job("scalar", expj, waits=[t_prep], inc=1)

    def smj(o, sem, inc):
        o.memset(p.msg_sb[:, 16:1040], 0.0)
        for j in range(NCH):
            for ft in range(NFT):
                inst = o.scalar_tensor_tensor(
                    p.msg_sb[:, 16 + ft * 64:16 + (ft + 1) * 64],
                    p.sloc[:, j * 1024 + ft * 64:j * 1024 + (ft + 1) * 64],
                    p.cb_t1[:, ft * 8 + j:ft * 8 + j + 1],
                    p.msg_sb[:, 16 + ft * 64:16 + (ft + 1) * 64],
                    Alu.mult, Alu.add)
        inst.then_inc(sem, inc)
    t_sm = S.job("vector", smj, waits=[t_exp], inc=1)

    def dmaj(o, sem, inc):
        o.dma_start(out=p.msg_in, in_=p.msg_sb[:, :]).then_inc(sem, 16)
        if p.debug:
            o.dma_start(out=p.dbg_msgin, in_=p.msg_sb[:, :]).then_inc(sem, 16)
            o.dma_start(out=p.dbg_clm, in_=p.cLcM[:, :]).then_inc(sem, 16)
            o.dma_start(out=p.dbg_D, in_=p.cb_t1[:, :]).then_inc(sem, 16)
    t_dma = S.job("sync", dmaj, waits=[t_sm],
                  inc=16 * (4 if p.debug else 1), dma=True)

    if getattr(p, "no_cc", False):
        def fakeag(o, sem, inc):
            for cc in range(NCORES):
                o.dma_start(out=p.msg_all[cc], in_=p.msg_sb[:, :]).then_inc(sem, 16)
        p.t_mag = S.job("sync", fakeag, waits=[t_sm], inc=16 * 8, dma=True)
        return

    def agj(o, sem, inc):
        o.collective_compute(
            "AllGather", Alu.bypass, replica_groups=[list(range(NCORES))],
            ins=[p.msg_in.opt()], outs=[p.msg_all.opt()]).then_inc(sem, inc)
    p.t_mag = S.job("gpsimd", agj, waits=[t_dma], inc=1, cc=True)


def ph_combine(p):
    S = p.s

    def lj(o, sem, inc):
        for cc in range(NCORES):
            o.dma_start(out=p.cb_msgs[:, cc * 1040:(cc + 1) * 1040],
                        in_=p.msg_all[cc]).then_inc(sem, 16)
        if p.debug:
            for cc in range(NCORES):
                o.dma_start(out=p.dbg_msg[cc],
                            in_=p.cb_msgs[:, cc * 1040:(cc + 1) * 1040]) \
                    .then_inc(sem, 16)
    t_l = S.job("sync", lj, waits=[p.t_mag], inc=16 * (8 + (8 if p.debug else 0)),
                dma=True)

    def zj(o, sem, inc):
        o.memset(p.S0A[:, :], 0.0)
        o.memset(p.S0B[:, :], 0.0).then_inc(sem, inc)
    t_z = S.job("gpsimd", zj, inc=1)

    def cj(o, sem, inc):
        for cc in range(NCORES - 1):
            o.tensor_scalar(p.cb_ae[:, :], p.cb_msgs[:, cc * 1040:cc * 1040 + 16],
                            p.pm_sb[:, cc:cc + 1], p.pmi_sb[:, cc:cc + 1],
                            Alu.mult, Alu.add)
            o.tensor_scalar_mul(p.cb_sm[:, :],
                                p.cb_msgs[:, cc * 1040 + 16:(cc + 1) * 1040],
                                p.pm_sb[:, cc:cc + 1])
            for ft in range(NFT):
                inst = o.scalar_tensor_tensor(
                    p.srun[:, ft * 64:(ft + 1) * 64],
                    p.srun[:, ft * 64:(ft + 1) * 64],
                    p.cb_ae[:, ft:ft + 1],
                    p.cb_sm[:, ft * 64:(ft + 1) * 64],
                    Alu.mult, Alu.add)
        inst.then_inc(sem, inc)
    t_c = S.job("vector", cj, waits=[t_l], inc=1)
    S.barrier()
    return t_c, t_z


def st_scan(p, sc, t_r, t_g):
    S = p.s
    py = pb(p, 0, cols=2048)
    psE = pb(p, 4)
    psO = pb(p, 5)
    pa = [pb(p, 6), pb(p, 7)]

    t_ld = {}
    t_til = {}
    t_ye_prev = ("sync", 0)
    t_z_prev = ("sync", 0)
    t_sr_prev = ("sync", 0)
    t_ym3_prev = ("sync", 0)
    t_dbg_prev = ("sync", 0)
    t_dbgall_prev = []
    for cj in range(NCHS):
        j = sc * NCHS + cj
        co = (cj % 2) * C
        snap = S.tickets()
        cfence_t = [("vector", snap["vector"]), ("scalar", snap["scalar"])]
        cfence_v = [("tensor", snap["tensor"]), ("scalar", snap["scalar"])]
        cfence_s = [("tensor", snap["tensor"]), ("vector", snap["vector"])]

        def ldj(o, sem, inc, cj=cj, j=j, co=co):
            for spill, buf in ((p.spill_k, p.kch), (p.spill_v, p.vch),
                               (p.spill_e, p.ech)):
                o.dma_start(out=buf[:, co:co + C],
                            in_=spill[:, j * C:(j + 1) * C]).then_inc(sem, 16)
        t_ld[cj] = S.job("sync", ldj,
                         waits=[t_til.get(cj - 2, ("sync", 0)),
                                t_ye_prev if cj >= 2 else ("sync", 0),
                                t_r, t_g],
                         inc=48, dma=True)

        t_cm = {}
        t_cme = {}
        for mtx, (mat, dst) in enumerate(((p.MK, p.cmk), (p.MR, p.cmr))):
            for n in range(4):
                i = mtx * 4 + n

                def tj(o, sem, inc, mat=mat, n=n, co=co, i=i):
                    o.matmul(pa[i % 2][:, :], mat,
                             p.ech[:, co + n * 512:co + (n + 1) * 512],
                             start=True, stop=True).then_inc(sem, inc)
                t_cm[i] = S.job("tensor", tj,
                                waits=[t_ld[cj], t_cme.get(i - 2, ("sync", 0)),
                                       t_z_prev] + (cfence_t if i == 0 else []),
                                inc=1)

                def ej(o, sem, inc, dst=dst, n=n, i=i):
                    o.tensor_copy(out=dst[:, n * 512:(n + 1) * 512],
                                  in_=pa[i % 2][:, :]).then_inc(sem, inc)
                t_cme[i] = S.job("vector", ej,
                                 waits=[t_cm[i]] +
                                 ((t_dbgall_prev + cfence_v) if i == 0 else []),
                                 inc=1)

        t_fe = {}
        t_tr = {}
        for g in range(8):
            src = p.cmk if g < 4 else p.cmr
            gg = g % 4

            def trj(o, sem, inc, src=src, gg=gg, g=g):
                last = None
                for fl in range(4):
                    last = o.transpose(
                        pa[g % 2][:, fl * 128:(fl + 1) * 128],
                        src[:, (gg * 4 + fl) * 128:(gg * 4 + fl + 1) * 128],
                        p.I128F)
                last.then_inc(sem, inc)
            t_tr[g] = S.job("tensor", trj,
                            waits=[t_cme[7], t_fe.get(g - 2, ("sync", 0))], inc=1)

            def fej(o, sem, inc, g=g, gg=gg):
                dst = p.Fk if g < 4 else p.Fr
                o.activation(out=dst[:, gg * 512:(gg + 1) * 512],
                             in_=pa[g % 2][:, :], func=Act.Exp,
                             scale=(1.0 if g < 4 else -1.0)).then_inc(sem, inc)
            t_fe[g] = S.job("scalar", fej,
                            waits=[t_tr[g]] + (cfence_s if g == 0 else []), inc=1)

        def tilj(o, sem, inc, cj=cj, co=co):
            o.tensor_mul(p.ktil[:, :], p.kch[:, co:co + C], p.Fk[:, :])
            o.tensor_mul(v3(p.rtil, 128)[:, :, :],
                         v3(p.ksc, SCT)[:, :, cj * 128:(cj + 1) * 128],
                         v3(p.Fr, 128)[:, :, :])
            o.tensor_mul(v3(p.rku, 128)[:, :, :],
                         v3(p.ksc, SCT)[:, :, cj * 128:(cj + 1) * 128],
                         v3(p.kch[:, co:co + C], 128)[:, :, :])
            for ft in range(NFT):
                inst = o.tensor_scalar_mul(p.rku[:, ft * 128:(ft + 1) * 128],
                                           p.rku[:, ft * 128:(ft + 1) * 128],
                                           p.u_sb[:, ft:ft + 1])
            inst.then_inc(sem, inc)
        t_til[cj] = S.job("vector", tilj, waits=[t_fe[7], t_r], inc=1)

        def dj(o, sem, inc):
            last = None
            for ft in range(NFT):
                last = o.matmul(pa[0][:, 2 * ft:2 * ft + 2],
                                p.rku[:, ft * 128:(ft + 1) * 128], p.BH2,
                                start=(ft == 0), stop=(ft == NFT - 1))
            last.then_inc(sem, inc)
        t_dm = S.job("tensor", dj, waits=[t_til[cj], t_fe[3]], inc=1)

        def dej(o, sem, inc):
            o.tensor_copy(out=p.dtm[:, :], in_=pa[0][:, 0:32]).then_inc(sem, inc)
        t_de = S.job("vector", dej, waits=[t_dm], inc=1)

        t_sco = {}
        t_sce = {}
        for grp in range(4):
            def scj(o, sem, inc, grp=grp):
                last = None
                for fl in range(4):
                    ft = grp * 4 + fl
                    o.matmul(psE[:, fl * 128:(fl + 1) * 128],
                             p.ktil[0:64, ft * 128:(ft + 1) * 128],
                             p.rtil[0:64, ft * 128:(ft + 1) * 128],
                             start=True, stop=True, tile_position=(0, 0))
                    last = o.matmul(psO[:, fl * 128:(fl + 1) * 128],
                                    p.ktil[64:128, ft * 128:(ft + 1) * 128],
                                    p.rtil[64:128, ft * 128:(ft + 1) * 128],
                                    start=True, stop=True, tile_position=(64, 0))
                last.then_inc(sem, inc)
            t_sco[grp] = S.job("tensor", scj,
                               waits=[t_til[cj], t_sce.get(grp - 1, ("sync", 0))],
                               inc=1)

            def scej(o, sem, inc, grp=grp):
                for fl in range(4):
                    ft = grp * 4 + fl
                    hE, hO = 2 * ft, 2 * ft + 1
                    o.tensor_mul(p.scT[:, hE * 128:(hE + 1) * 128],
                                 psE[:, fl * 128:(fl + 1) * 128], p.MASKU)
                    inst = o.tensor_mul(p.scT[:, hO * 128:(hO + 1) * 128],
                                        psO[:, fl * 128:(fl + 1) * 128], p.MASKU)
                inst.then_inc(sem, inc)
            t_sce[grp] = S.job("vector", scej, waits=[t_sco[grp]], inc=1)

        def s0j(o, sem, inc, j=j):
            for ft in range(NFT):
                o.tensor_scalar_mul(p.S0A[0:64, ft * 64:(ft + 1) * 64],
                                    p.srun[0:64, ft * 64:(ft + 1) * 64],
                                    p.enm[0:64, ft * 8 + j:ft * 8 + j + 1])
                inst = o.tensor_scalar_mul(
                    p.S0B[64:128, ft * 64:(ft + 1) * 64],
                    p.srun[64:128, ft * 64:(ft + 1) * 64],
                    p.enm[64:128, ft * 8 + j:ft * 8 + j + 1])
            inst.then_inc(sem, inc)
        t_s0 = S.job("vector", s0j, waits=[t_sr_prev, t_ym3_prev], inc=1)

        def srj(o, sem, inc, j=j):
            for ft in range(NFT):
                inst = o.scalar_tensor_tensor(
                    p.srun[:, ft * 64:(ft + 1) * 64],
                    p.srun[:, ft * 64:(ft + 1) * 64],
                    p.A_ch[:, ft * 8 + j:ft * 8 + j + 1],
                    p.sloc[:, j * 1024 + ft * 64:j * 1024 + (ft + 1) * 64],
                    Alu.mult, Alu.add)
            inst.then_inc(sem, inc)
        t_sr_prev = S.job("vector", srj, waits=[t_s0], inc=1)

        t_ym = {}
        for bank in range(4):
            def yj(o, sem, inc, bank=bank, co=co):
                last = None
                for fl in range(4):
                    ft = bank * 4 + fl
                    hE, hO = 2 * ft, 2 * ft + 1
                    o.matmul(py[:, ft * 128:ft * 128 + 64],
                             p.scT[:, hE * 128:(hE + 1) * 128],
                             p.vch[:, co + ft * 128:co + ft * 128 + 64],
                             start=(fl == 0), stop=False)
                    o.matmul(py[:, ft * 128 + 64:(ft + 1) * 128],
                             p.scT[:, hO * 128:(hO + 1) * 128],
                             p.vch[:, co + ft * 128 + 64:co + (ft + 1) * 128],
                             start=False, stop=False)
                    o.matmul(py[:, ft * 128:ft * 128 + 64],
                             p.rtil[:, ft * 128:(ft + 1) * 128],
                             p.S0A[:, ft * 64:(ft + 1) * 64],
                             start=False, stop=False)
                    last = o.matmul(py[:, ft * 128 + 64:(ft + 1) * 128],
                                    p.rtil[:, ft * 128:(ft + 1) * 128],
                                    p.S0B[:, ft * 64:(ft + 1) * 64],
                                    start=False, stop=(fl == 3))
                last.then_inc(sem, inc)
            t_ym[bank] = S.job("tensor", yj,
                               waits=[t_sce[bank], t_s0, t_ye_prev], inc=1)

        def yev(o, sem, inc):
            o.tensor_copy(out=p.ysb[:, :], in_=py[:, :]).then_inc(sem, inc)
        t_yev = S.job("vector", yev, waits=[t_ym[3], t_dbg_prev], inc=1)
        t_dbgy = None
        if p.debug:
            def dbgyraw(o, sem, inc, j=j):
                o.dma_start(out=p.dbg_yraw[:, j * C:(j + 1) * C],
                            in_=p.ysb[:, :]).then_inc(sem, 16)
            t_dbgy = S.job("sync", dbgyraw, waits=[t_yev], inc=16, dma=True)

        def yej(o, sem, inc, co=co):
            for h in range(H):
                inst = o.scalar_tensor_tensor(
                    p.ysb[:, h * 64:(h + 1) * 64],
                    p.vch[:, co + h * 64:co + (h + 1) * 64],
                    p.dtm[:, h:h + 1],
                    p.ysb[:, h * 64:(h + 1) * 64],
                    Alu.mult, Alu.add)
            inst.then_inc(sem, inc)
        t_ye = S.job("vector", yej,
                     waits=[t_yev, t_de] + ([t_dbgy] if t_dbgy else []), inc=1)
        if p.debug:
            def dbgydv(o, sem, inc, j=j):
                o.dma_start(out=p.dbg_ydv[:, j * C:(j + 1) * C],
                            in_=p.ysb[:, :]).then_inc(sem, 16)
            t_dbgydv = S.job("sync", dbgydv, waits=[t_ye], inc=16, dma=True)
        t_ye_prev = t_ye
        t_ym3_prev = t_ym[3]

        def sqj(o, sem, inc):
            o.square(out=p.cmk[:, :], in_=p.ysb[:, :]).then_inc(sem, inc)
        t_sq = S.job("scalar", sqj, waits=[t_ye], inc=1)

        def st1j(o, sem, inc):
            o.reduce_sum(p.gstat[:, 0:32], v3(p.ysb, 64), axis=mybir.AxisListType.X)
            o.reduce_sum(p.gstat[:, 32:64], v3(p.cmk, 64), axis=mybir.AxisListType.X)
            o.tensor_scalar(p.gstat[:, 64:96], p.gstat[:, 0:32],
                            1.0 / 64.0, None, Alu.mult)
            o.tensor_scalar(p.gstat[:, 96:128], p.gstat[:, 32:64],
                            1.0 / 64.0, None, Alu.mult)
            o.tensor_mul(p.gstat[:, 128:160], p.gstat[:, 64:96], p.gstat[:, 64:96])
            o.tensor_sub(p.gstat[:, 0:32], p.gstat[:, 96:128],
                         p.gstat[:, 128:160]).then_inc(sem, inc)
        t_st1 = S.job("vector", st1j, waits=[t_ye, t_sq], inc=1)

        def st2j(o, sem, inc):
            o.activation(out=p.gstat[:, 160:192], in_=p.gstat[:, 0:32],
                         func=Act.Sqrt, bias=p.epsb[:, 0:1]).then_inc(sem, inc)
        t_st2 = S.job("scalar", st2j, waits=[t_st1], inc=1)

        def st3a(o, sem, inc):
            o.reciprocal(p.gstat[:, 192:224],
                         p.gstat[:, 160:192]).then_inc(sem, inc)
        t_s3a = S.job("vector", st3a, waits=[t_st2], inc=1)

        def st3b(o, sem, inc):
            o.copy(out=p.gstat[:, 96:128],
                   in_=p.gstat[:, 192:224]).then_inc(sem, inc)
        t_s3b = S.job("scalar", st3b, waits=[t_s3a], inc=1)

        def st3c(o, sem, inc):
            o.scalar_tensor_tensor(p.gstat[:, 224:256], p.gstat[:, 64:96],
                                   -1.0, p.gstat[:, 96:128],
                                   Alu.mult, Alu.mult).then_inc(sem, inc)
        t_st3 = S.job("vector", st3c, waits=[t_s3b], inc=1)

        def gaj(o, sem, inc):
            for h in range(H):
                inst = o.scalar_tensor_tensor(
                    p.ysb[:, h * 64:(h + 1) * 64],
                    p.ysb[:, h * 64:(h + 1) * 64],
                    p.gstat[:, 96 + h:97 + h],
                    p.gstat[:, 224 + h:225 + h].broadcast_to((128, 64)),
                    Alu.mult, Alu.add)
            inst.then_inc(sem, inc)
        t_ga = S.job("vector", gaj,
                     waits=[t_st3] + ([t_dbgydv] if p.debug else []), inc=1)
        if p.debug:
            def dbggst(o, sem, inc, j=j):
                o.dma_start(out=p.dbg_gst[:, j * 256:(j + 1) * 256],
                            in_=p.gstat[:, :]).then_inc(sem, 16)
            t_dbggst = S.job("sync", dbggst, waits=[t_st3, t_ga], inc=16, dma=True)

        t_yt = {}
        t_z1 = {}
        t_zz = {}
        for grp in range(4):
            def ytj(o, sem, inc, grp=grp):
                last = None
                for fl in range(4):
                    ft = grp * 4 + fl
                    last = o.transpose(pa[grp % 2][:, fl * 128:(fl + 1) * 128],
                                       p.ysb[:, ft * 128:(ft + 1) * 128], p.I128F)
                last.then_inc(sem, inc)
            t_yt[grp] = S.job("tensor", ytj,
                              waits=[t_ga, t_z1.get(grp - 2, ("sync", 0))], inc=1)

            def z1j(o, sem, inc, grp=grp):
                for fl in range(4):
                    ft = grp * 4 + fl
                    inst = o.scalar_tensor_tensor(
                        p.tmpA[:, fl * 128:(fl + 1) * 128],
                        pa[grp % 2][:, fl * 128:(fl + 1) * 128],
                        p.lnw_sb[:, ft:ft + 1],
                        p.lnb_sb[:, ft:ft + 1].broadcast_to((128, 128)),
                        Alu.mult, Alu.add)
                inst.then_inc(sem, inc)
            t_z1[grp] = S.job("vector", z1j,
                              waits=[t_yt[grp], t_zz.get(grp - 1, ("sync", 0))], inc=1)

            def zzj(o, sem, inc, grp=grp, cj=cj):
                o.tensor_mul(
                    v3(p.zsc, SCT)[:, grp * 4:(grp + 1) * 4,
                                   cj * 128:(cj + 1) * 128],
                    v3(p.tmpA[:, 0:512], 128)[:, :, :],
                    v3(p.gsc, SCT)[:, grp * 4:(grp + 1) * 4,
                                   cj * 128:(cj + 1) * 128]).then_inc(sem, inc)
            t_zz[grp] = S.job("vector", zzj, waits=[t_z1[grp], t_g], inc=1)
        t_z_prev = t_zz[3]

        if p.debug:
            def dbgj(o, sem, inc, j=j):
                o.dma_start(out=p.dbg_y[:, j * C:(j + 1) * C],
                            in_=p.ysb[:, :]).then_inc(sem, 16)
                o.dma_start(out=p.dbg_srun[:, j * 1024:(j + 1) * 1024],
                            in_=p.srun[:, :]).then_inc(sem, 16)
            t_dbg1 = S.job("sync", dbgj, waits=[t_ga, t_sr_prev], inc=32,
                           dma=True)

            def dbg2(o, sem, inc, j=j):
                o.dma_start(out=p.dbg_sct[:, j * 4096:(j + 1) * 4096],
                            in_=p.scT[:, :]).then_inc(sem, 16)
                o.dma_start(out=p.dbg_s0[:, j * 2048:j * 2048 + 1024],
                            in_=p.S0A[:, :]).then_inc(sem, 16)
                o.dma_start(out=p.dbg_s0[:, j * 2048 + 1024:(j + 1) * 2048],
                            in_=p.S0B[:, :]).then_inc(sem, 16)
                o.dma_start(out=p.dbg_dtm[:, j * 32:(j + 1) * 32],
                            in_=p.dtm[:, :]).then_inc(sem, 16)
                o.dma_start(out=p.dbg_til[:, j * 4096:j * 4096 + 2048],
                            in_=p.ktil[:, :]).then_inc(sem, 16)
                o.dma_start(out=p.dbg_til[:, j * 4096 + 2048:(j + 1) * 4096],
                            in_=p.rtil[:, :]).then_inc(sem, 16)
            t_dbg2 = S.job("gpsimd", dbg2, waits=[t_sce[3], t_s0, t_de,
                                                  t_til[cj], t_ym[3]],
                           inc=96, dma=True)
            t_dbg_prev = t_dbg1
            t_dbgall_prev = [t_dbg1, t_dbg2, t_dbgydv, t_dbggst, t_dbgy]

    return t_z_prev


def st_wo_ln1(p, sc, t_z, t_gdone=("sync", 0)):
    S = p.s
    po = [pb(p, 4), pb(p, 5)]
    psu = pb(p, 6, rows=1)
    psq = pb(p, 7, rows=1)
    pbc = pb(p, 0, cols=1024)

    t_w = {}
    t_mm = {}
    t_r1 = {}
    t_sq = {}
    t_st = {}
    t_xd = {}
    for mt in range(NFT):
        def wj(o, sem, inc, mt=mt):
            o.dma_start(out=v3(p.wsl[:, (mt % 2) * C:(mt % 2 + 1) * C], 128),
                        in_=p.w_l["o"][mt]).then_inc(sem, inc)
        t_w[mt] = S.job("gpsimd", wj,
                        waits=[t_mm.get(mt - 2, t_gdone)], inc=16, dma=True)

        def xj(o, sem, inc, mt=mt):
            o.dma_start(out=p.xst[:, (mt % 2) * 513:(mt % 2) * 513 + 512],
                        in_=p.xT[mt * 128:(mt + 1) * 128,
                                 sc * SCT + 1:(sc + 1) * SCT + 1]).then_inc(sem, inc)
        t_xd[mt] = S.job("sync", xj, waits=[t_r1.get(mt - 2, ("sync", 0))], inc=16, dma=True)

        def tj(o, sem, inc, mt=mt):
            last = None
            for kt in range(NFT):
                last = o.matmul(
                    po[mt % 2][:, :],
                    p.wsl[:, (mt % 2) * C + kt * 128:(mt % 2) * C + (kt + 1) * 128],
                    p.zsc[:, kt * SCT:(kt + 1) * SCT],
                    start=(kt == 0), stop=(kt == NFT - 1))
            last.then_inc(sem, inc)
        t_mm[mt] = S.job("tensor", tj,
                         waits=[t_w[mt], t_z, t_r1.get(mt - 2, ("sync", 0))], inc=1)

        def rj(o, sem, inc, mt=mt):
            o.tensor_add(p.r1[:, mt * SCT:(mt + 1) * SCT], po[mt % 2][:, :],
                         p.xst[:, (mt % 2) * 513:(mt % 2) * 513 + 512]) \
                .then_inc(sem, inc)
        t_r1[mt] = S.job("vector", rj, waits=[t_mm[mt], t_xd[mt]], inc=1)

        def sqj(o, sem, inc, mt=mt):
            o.square(out=p.r1sq[:, (mt % 2) * 512:(mt % 2 + 1) * 512],
                     in_=p.r1[:, mt * SCT:(mt + 1) * SCT]).then_inc(sem, inc)
        t_sq[mt] = S.job("scalar", sqj,
                         waits=[t_r1[mt], t_st.get(mt - 2, ("sync", 0))], inc=1)

        def stj(o, sem, inc, mt=mt):
            o.matmul(psu[:, :], p.ONES1, p.r1[:, mt * SCT:(mt + 1) * SCT],
                     start=(mt == 0), stop=(mt == NFT - 1))
            o.matmul(psq[:, :], p.ONES1,
                     p.r1sq[:, (mt % 2) * 512:(mt % 2 + 1) * 512],
                     start=(mt == 0), stop=(mt == NFT - 1)).then_inc(sem, inc)
        t_st[mt] = S.job("tensor", stj, waits=[t_sq[mt]], inc=1)

    def rowj(o, sem, inc):
        o.tensor_scalar(p.rows[:, 0:512], psu[:, :], 1.0 / C, None, Alu.mult)
        o.tensor_scalar(p.rows[:, 512:1024], psq[:, :], 1.0 / C, None, Alu.mult)
        o.tensor_mul(p.rows[:, 1024:1536], p.rows[:, 0:512], p.rows[:, 0:512])
        o.tensor_sub(p.rows[:, 512:1024], p.rows[:, 512:1024],
                     p.rows[:, 1024:1536]).then_inc(sem, inc)
    t_row = S.job("vector", rowj, waits=[t_st[NFT - 1]], inc=1)

    def rsj(o, sem, inc):
        o.activation(out=p.rows[:, 1024:1536], in_=p.rows[:, 512:1024],
                     func=Act.Sqrt, bias=p.epsb[0:1, 1:2]).then_inc(sem, inc)
    t_rs = S.job("scalar", rsj, waits=[t_row], inc=1)

    def rcj(o, sem, inc):
        o.reciprocal(p.rows[:, 512:1024], p.rows[:, 1024:1536]).then_inc(sem, inc)
    t_rc = S.job("vector", rcj, waits=[t_rs], inc=1)

    def bcj(o, sem, inc):
        o.matmul(pbc[:, 0:512], p.ONESR, p.rows[:, 0:512], start=True, stop=True)
        o.matmul(pbc[:, 512:1024], p.ONESR, p.rows[:, 512:1024],
                 start=True, stop=True).then_inc(sem, inc)
    t_bc = S.job("tensor", bcj, waits=[t_rc], inc=1)

    t_x1 = {}
    t_x1d = {}
    for mt in range(NFT):
        def aj(o, sem, inc, mt=mt):
            o.tensor_sub(p.r1[:, mt * SCT:(mt + 1) * SCT],
                         p.r1[:, mt * SCT:(mt + 1) * SCT], pbc[:, 0:512])
            o.tensor_mul(p.r1[:, mt * SCT:(mt + 1) * SCT],
                         p.r1[:, mt * SCT:(mt + 1) * SCT],
                         pbc[:, 512:1024]).then_inc(sem, inc)
        t_x1[mt] = S.job("vector", aj, waits=[t_bc], inc=1)

        def dj(o, sem, inc, mt=mt):
            o.dma_start(out=p.spill_x1[:, mt * TC + sc * SCT:mt * TC + (sc + 1) * SCT],
                        in_=p.r1[:, mt * SCT:(mt + 1) * SCT]).then_inc(sem, inc)
        t_x1d[mt] = S.job("sync", dj, waits=[t_x1[mt]], inc=16, dma=True)
    return t_x1d[NFT - 1]


def ph_pass2_sc(p, sc):
    S = p.s
    xdone = st_xload(p, sc)
    t_m = st_mstage(p, [3, 4], xdone)
    t_mixr, t_ta_r = st_mix(p, 3, p.mixB, t_m[3], 0)

    def evr(o, mt, ps, sem, inc):
        o.tensor_copy(out=p.ksc[:, mt * SCT:(mt + 1) * SCT],
                      in_=ps).then_inc(sem, inc)
    t_re = st_proj_fm(p, "r", p.mixB, t_mixr, evr,
                      fence=[t_m[3], t_m[4]])
    t_mixg, t_ta_g = st_mix(p, 4, p.mixA, t_m[4], 1, fence=[t_ta_r])

    def evg(o, mt, ps, sem, inc):
        o.activation(out=p.gsc[:, mt * SCT:(mt + 1) * SCT], in_=ps,
                     func=Act.Silu).then_inc(sem, inc)
    t_ge = st_proj_fm(p, "g", p.mixA, t_mixg, evg, eng_ev="scalar",
                      fence=[t_re[NFT - 1]])
    t_z = st_scan(p, sc, t_re[NFT - 1], t_ge[NFT - 1])
    t_x1 = st_wo_ln1(p, sc, t_z, t_gdone=t_ge[NFT - 1])
    if p.debug:
        def dbgz(o, sem, inc):
            for ft in range(NFT):
                o.dma_start(
                    out=p.dbg_z[:, ft * TC + sc * SCT:ft * TC + (sc + 1) * SCT],
                    in_=p.zsc[:, ft * SCT:(ft + 1) * SCT]).then_inc(sem, 16)
        p.s.job("gpsimd", dbgz, waits=[t_z], inc=16 * NFT, dma=True)
    return t_x1


def ph_ffn_sc(p, sc, t_qag):
    S = p.s
    pgu = [pb(p, 0), pb(p, 1)]
    pss = pb(p, 2, rows=1)

    t_xc = {}
    for kt in range(NFT):
        def xj(o, sem, inc, kt=kt):
            o.dma_start(out=p.f_x1t[:, (kt % 2) * SCT:(kt % 2 + 1) * SCT],
                        in_=p.spill_x1[:, kt * TC + sc * SCT:kt * TC + (sc + 1) * SCT]) \
                .then_inc(sem, inc)
        t_xd = S.job("sync", xj, waits=[t_xc.get(kt - 2, ("sync", 0))], inc=16, dma=True)

        def cjb(o, sem, inc, kt=kt):
            o.tensor_copy(out=p.f_x1b[:, kt * SCT:(kt + 1) * SCT],
                          in_=p.f_x1t[:, (kt % 2) * SCT:(kt % 2 + 1) * SCT]) \
                .then_inc(sem, inc)
        t_xc[kt] = S.job("vector", cjb, waits=[t_xd], inc=1)
    t_x1b = t_xc[NFT - 1]

    order = [x for i in range(64) for x in (i, 64 + i)]
    t_w = {}
    t_mm = {}
    t_hv = {}
    t_hs = {}
    t_ss = {}
    for idx, mt in enumerate(order):
        def wj(o, sem, inc, mt=mt, idx=idx):
            o.dma_start(
                out=v3(p.f_gusl[:, (idx % 2) * C:(idx % 2 + 1) * C], 128),
                in_=p.qgu_all[mt // 16, mt % 16]).then_inc(sem, inc)
        t_w[idx] = S.job("gpsimd", wj,
                         waits=[t_qag, t_mm.get(idx - 2, ("sync", 0))], inc=16,
                         dma=True)

        def tj(o, sem, inc, mt=mt, idx=idx):
            last = None
            for kt in range(NFT):
                last = o.matmul(
                    pgu[idx % 2][:, :],
                    p.f_gusl[:, (idx % 2) * C + kt * 128:(idx % 2) * C + (kt + 1) * 128],
                    p.f_x1b[:, kt * SCT:(kt + 1) * SCT],
                    start=(kt == 0), stop=(kt == NFT - 1))
            last.then_inc(sem, inc)
        gate_pair = t_hv.get(idx - 2, ("sync", 0))
        t_mm[idx] = S.job("tensor", tj,
                          waits=[t_w[idx], t_x1b,
                                 t_hs.get(idx - 2, ("sync", 0)), gate_pair], inc=1)

        if idx % 2 == 0:
            def gj(o, sem, inc, idx=idx):
                o.activation(out=p.f_sg[:, :], in_=pgu[idx % 2][:, :],
                             func=Act.Silu).then_inc(sem, inc)
            t_hs[idx] = S.job("scalar", gj, waits=[t_mm[idx]], inc=1)
        else:
            i = idx // 2

            def hj(o, sem, inc, i=i, idx=idx):
                o.tensor_mul(p.f_h[:, i * SCT:(i + 1) * SCT], pgu[idx % 2][:, :],
                             p.f_sg[:, :]).then_inc(sem, inc)
            t_hv[idx] = S.job("vector", hj, waits=[t_mm[idx], t_hs[idx - 1]], inc=1)

            def h2j(o, sem, inc, i=i):
                o.square(out=p.f_h2[:, (i % 2) * SCT:(i % 2 + 1) * SCT],
                         in_=p.f_h[:, i * SCT:(i + 1) * SCT]).then_inc(sem, inc)
            t_hs[idx] = S.job("scalar", h2j,
                              waits=[t_hv[idx], t_ss.get(idx - 2, ("sync", 0))],
                              inc=1)

            def ssj(o, sem, inc, i=i):
                o.matmul(pss[:, :], p.ONES1,
                         p.f_h2[:, (i % 2) * SCT:(i % 2 + 1) * SCT],
                         start=(i == 0), stop=(i == 63)).then_inc(sem, inc)
            t_ss[idx] = S.job("tensor", ssj, waits=[t_hs[idx]], inc=1)

    def rmj(o, sem, inc):
        o.tensor_scalar(p.f_rows[:, 0:512], pss[:, :], 1.0 / 8192.0,
                        None, Alu.mult).then_inc(sem, inc)
    t_rm = S.job("vector", rmj, waits=[t_ss[127]], inc=1)

    def rsj(o, sem, inc):
        o.activation(out=p.f_rows[:, 1024:1536], in_=p.f_rows[:, 0:512],
                     func=Act.Sqrt, bias=p.epsb[0:1, 2:3]).then_inc(sem, inc)
    t_rs = S.job("scalar", rsj, waits=[t_rm], inc=1)

    def rrj(o, sem, inc):
        o.reciprocal(p.f_rows[:, 512:1024], p.f_rows[:, 1024:1536]) \
            .then_inc(sem, inc)
    t_rr = S.job("vector", rrj, waits=[t_rs], inc=1)

    pbc = pb(p, 3)

    def bcj(o, sem, inc):
        o.matmul(pbc[:, :], p.ONESR, p.f_rows[:, 512:1024],
                 start=True, stop=True).then_inc(sem, inc)
    t_bc = S.job("tensor", bcj, waits=[t_rr], inc=1)

    def bcv(o, sem, inc):
        o.tensor_copy(out=p.f_rstdb[:, :], in_=pbc[:, :]).then_inc(sem, inc)
    t_bcv = S.job("vector", bcv, waits=[t_bc], inc=1)

    ppr = [pb(p, 4), pb(p, 5)]
    psu2 = pb(p, 6, rows=1)
    psq2 = pb(p, 7, rows=1)
    t_pw = {}
    t_pm = {}
    t_r2 = {}
    t_q2 = {}
    t_s2 = {}
    t_xt = {}
    for mt in range(NFT):
        for half in range(2):
            g = mt * 2 + half

            def wj(o, sem, inc, mt=mt, half=half, g=g):
                o.dma_start(
                    out=v3(p.f_prsl[:, (g % 2) * 4096:(g % 2 + 1) * 4096], 128),
                    in_=p.qpr_all[mt // 2, mt % 2][:, half * 32:(half + 1) * 32, :]) \
                    .then_inc(sem, inc)
            t_pw[g] = S.job("gpsimd", wj,
                            waits=[t_qag, t_pm.get((g - 2) // 2, ("sync", 0))
                                   if g >= 2 else ("sync", 0)], inc=16, dma=True)

        def xj(o, sem, inc, mt=mt):
            o.dma_start(out=p.f_x1t[:, (mt % 2) * SCT:(mt % 2 + 1) * SCT],
                        in_=p.spill_x1[:, mt * TC + sc * SCT:mt * TC + (sc + 1) * SCT]) \
                .then_inc(sem, inc)
        t_xt[mt] = S.job("sync", xj,
                         waits=[t_x1b, t_r2.get(mt - 2, ("sync", 0))], inc=16,
                         dma=True)

        def tj(o, sem, inc, mt=mt):
            last = None
            for kt in range(64):
                g = mt * 2 + kt // 32
                b = (g % 2) * 4096 + (kt % 32) * 128
                last = o.matmul(
                    ppr[mt % 2][:, :], p.f_prsl[:, b:b + 128],
                    p.f_h[:, kt * SCT:(kt + 1) * SCT],
                    start=(kt == 0), stop=(kt == 63))
            last.then_inc(sem, inc)
        t_pm[mt] = S.job("tensor", tj,
                         waits=[t_pw[mt * 2], t_pw[mt * 2 + 1],
                                t_hv.get(127, ("sync", 0)),
                                t_r2.get(mt - 2, ("sync", 0))], inc=1)

        def rj(o, sem, inc, mt=mt):
            o.tensor_mul(p.f_r2[:, mt * SCT:(mt + 1) * SCT], ppr[mt % 2][:, :],
                         p.f_rstdb[:, :])
            o.tensor_add(p.f_r2[:, mt * SCT:(mt + 1) * SCT],
                         p.f_r2[:, mt * SCT:(mt + 1) * SCT],
                         p.f_x1t[:, (mt % 2) * SCT:(mt % 2 + 1) * SCT]) \
                .then_inc(sem, inc)
        t_r2[mt] = S.job("vector", rj, waits=[t_pm[mt], t_xt[mt], t_bcv], inc=1)

        def qj(o, sem, inc, mt=mt):
            o.square(out=p.f_h2[:, (mt % 2) * SCT:(mt % 2 + 1) * SCT],
                     in_=p.f_r2[:, mt * SCT:(mt + 1) * SCT]).then_inc(sem, inc)
        t_q2[mt] = S.job("scalar", qj,
                         waits=[t_r2[mt], t_s2.get(mt - 2, ("sync", 0))], inc=1)

        def sj(o, sem, inc, mt=mt):
            o.matmul(psu2[:, :], p.ONES1, p.f_r2[:, mt * SCT:(mt + 1) * SCT],
                     start=(mt == 0), stop=(mt == NFT - 1))
            o.matmul(psq2[:, :], p.ONES1,
                     p.f_h2[:, (mt % 2) * SCT:(mt % 2 + 1) * SCT],
                     start=(mt == 0), stop=(mt == NFT - 1)).then_inc(sem, inc)
        t_s2[mt] = S.job("tensor", sj, waits=[t_q2[mt]], inc=1)

    def rowj(o, sem, inc):
        o.tensor_scalar(p.f_rows[:, 0:512], psu2[:, :], 1.0 / C, None, Alu.mult)
        o.tensor_scalar(p.f_rows[:, 512:1024], psq2[:, :], 1.0 / C, None, Alu.mult)
        o.tensor_mul(p.f_rows[:, 1024:1536], p.f_rows[:, 0:512], p.f_rows[:, 0:512])
        o.tensor_sub(p.f_rows[:, 512:1024], p.f_rows[:, 512:1024],
                     p.f_rows[:, 1024:1536]).then_inc(sem, inc)
    t_row = S.job("vector", rowj, waits=[t_s2[NFT - 1]], inc=1)

    def lsj(o, sem, inc):
        o.activation(out=p.f_rows[:, 1024:1536], in_=p.f_rows[:, 512:1024],
                     func=Act.Sqrt, bias=p.epsb[0:1, 1:2]).then_inc(sem, inc)
    t_ls = S.job("scalar", lsj, waits=[t_row], inc=1)

    def lrj(o, sem, inc):
        o.reciprocal(p.f_rows[:, 512:1024], p.f_rows[:, 1024:1536]) \
            .then_inc(sem, inc)
    t_lr = S.job("vector", lrj, waits=[t_ls], inc=1)

    pbc2 = pb(p, 2, cols=1024)

    def bc2j(o, sem, inc):
        o.matmul(pbc2[:, 0:512], p.ONESR, p.f_rows[:, 0:512], start=True, stop=True)
        o.matmul(pbc2[:, 512:1024], p.ONESR, p.f_rows[:, 512:1024],
                 start=True, stop=True).then_inc(sem, inc)
    t_bc2 = S.job("tensor", bc2j, waits=[t_lr], inc=1)

    t_o = {}
    for mt in range(NFT):
        def aj(o, sem, inc, mt=mt):
            o.tensor_sub(p.f_r2[:, mt * SCT:(mt + 1) * SCT],
                         p.f_r2[:, mt * SCT:(mt + 1) * SCT], pbc2[:, 0:512])
            o.tensor_mul(p.f_r2[:, mt * SCT:(mt + 1) * SCT],
                         p.f_r2[:, mt * SCT:(mt + 1) * SCT],
                         pbc2[:, 512:1024]).then_inc(sem, inc)
        t_a = S.job("vector", aj, waits=[t_bc2], inc=1)

        def oj(o, sem, inc, mt=mt):
            o.dma_start(out=p.out[mt * 128:(mt + 1) * 128, sc * SCT:(sc + 1) * SCT],
                        in_=p.f_r2[:, mt * SCT:(mt + 1) * SCT]).then_inc(sem, inc)
        t_o[mt] = S.job("sync", oj, waits=[t_a], inc=16, dma=True)
    return t_o[NFT - 1]


def plan(p, upto="full"):
    S = p.s
    ph_consts(p)
    if upto == "consts":
        return
    t_qag = None
    if upto in ("full", "ffn", "quant"):
        ph_quant(p)
        t_qag = p.t_qag
    if upto == "quant":
        if p.debug:
            def qdump(o, sem, inc):
                cid = 0
                for mtl in range(16):
                    for r8 in range(8):
                        o.dma_start(
                            out=p.dbg_qgu[mtl, :, r8 * 2:(r8 + 1) * 2, :],
                            in_=p.qgu_mine[mtl, :, r8 * 2:(r8 + 1) * 2, :]) \
                            .then_inc(sem, 16)
                for mtl in range(2):
                    for r8 in range(8):
                        o.dma_start(
                            out=p.dbg_qpr[mtl, :, r8 * 8:(r8 + 1) * 8, :],
                            in_=p.qpr_mine[mtl, :, r8 * 8:(r8 + 1) * 8, :]) \
                            .then_inc(sem, 16)
            S.job("gpsimd", qdump, waits=[t_qag], inc=16 * (128 + 16), dma=True)
        return
    p.pcl_ap = pb(p, 7, cols=384)
    for sc in range(2):
        ph_pass1_sc(p, sc, p.pcl_ap)
        S.barrier()
    if upto == "pass1":
        if p.debug:
            def dbgs(o, sem, inc):
                o.dma_start(out=p.dbg_sloc[:, :], in_=p.sloc[:, :]).then_inc(sem, 16)
            S.job("gpsimd", dbgs, inc=16, dma=True)
        return
    ph_msg(p)
    if upto == "msg":
        return
    ph_combine(p)
    if p.debug:
        def dbgs(o, sem, inc):
            o.dma_start(out=p.dbg_sloc[:, :], in_=p.sloc[:, :]).then_inc(sem, 16)
        S.job("gpsimd", dbgs, inc=16, dma=True)
    if upto == "combine":
        return
    for sc in range(2):
        ph_pass2_sc(p, sc)
        S.barrier()
    if upto == "pass2":
        return
    for sc in range(2):
        ph_ffn_sc(p, sc, t_qag)
        S.barrier()


def finish(p):
    p.s.barrier()


def make_nc(debug=False, upto="full", no_cc=False):
    p = build(debug=debug)
    if upto == "msg" or no_cc:
        p.no_cc = True
    plan(p, upto=upto)
    finish(p)
    with p.nc.Block() as block:
        p.s.emit(block)
    return p


def host_inputs(inputs):
    x = np.ascontiguousarray(np.asarray(inputs["L0_attn_x"], np.float32)[0].T)

    def w2l(w):
        return np.ascontiguousarray(
            np.asarray(w, np.float32).reshape(16, 128, 16, 128).transpose(2, 1, 0, 3))
    shared = {
        "wk_l": w2l(inputs["L0_attn_wk"]), "wr_l": w2l(inputs["L0_attn_wr"]),
        "wg_l": w2l(inputs["L0_attn_wg"]), "wo_l": w2l(inputs["L0_attn_wo"]),
        "wv_r": np.ascontiguousarray(np.asarray(inputs["L0_attn_wv"], np.float32)),
        "w1_l": np.ascontiguousarray(
            np.asarray(inputs["L0_attn_w1"], np.float32).reshape(16, 128, 160)),
        "w2_l": np.ascontiguousarray(np.asarray(inputs["L0_attn_w2"], np.float32)),
        "tdw1_l": np.ascontiguousarray(
            np.asarray(inputs["L0_attn_td_w1"], np.float32).reshape(16, 128, 64)),
        "tdw2_r": np.ascontiguousarray(np.asarray(inputs["L0_attn_td_w2"], np.float32)),
        "tdb": np.ascontiguousarray(np.broadcast_to(
            np.asarray(inputs["L0_attn_time_decay"], np.float32).reshape(1, C),
            (128, C))),
        "maa6": np.ascontiguousarray(np.stack(
            [np.asarray(inputs[f"L0_attn_time_maa_{n}"], np.float32).reshape(C)
             for n in "xwkvrg"], axis=1).reshape(16, 128, 6).transpose(1, 0, 2)),
        "u_h": np.ascontiguousarray(
            np.asarray(inputs["L0_attn_time_faaaa"], np.float32).reshape(16, 128).T),
        "lnw_h": np.ascontiguousarray(
            np.asarray(inputs["L0_attn_ln_w"], np.float32).reshape(16, 128).T),
        "lnb_h": np.ascontiguousarray(
            np.asarray(inputs["L0_attn_ln_b"], np.float32).reshape(16, 128).T),
    }
    wguT = np.ascontiguousarray(np.asarray(inputs["L0_ffn_w_gate_up"], np.float32).T)
    wprT = np.ascontiguousarray(np.asarray(inputs["L0_ffn_w_proj"], np.float32).T)
    in_maps = []
    for c in range(NCORES):
        prev = x[:, c * TC - 1:c * TC] if c > 0 else np.zeros((C, 1), np.float32)
        xTc = np.ascontiguousarray(
            np.concatenate([prev, x[:, c * TC:(c + 1) * TC]], axis=1))
        pmv = np.zeros((128, 8), np.float32)
        pmv[:, :c] = 1.0
        m = dict(shared)
        m["xT"] = xTc
        m["pm"] = pmv
        m["pmi"] = np.ascontiguousarray(1.0 - pmv)
        m["qgu_f32"] = np.ascontiguousarray(wguT[:, c * C:(c + 1) * C])
        m["qpr_f32"] = np.ascontiguousarray(wprT[:, c * 256:(c + 1) * 256])
        in_maps.append(m)
    return in_maps


def kernel(**inputs):
    in_maps = host_inputs(inputs)
    p = make_nc(debug=False, upto="full")
    res = run_bass_kernel_spmd(p.nc, in_maps, core_ids=list(range(NCORES)))
    outs = [r["out"] for r in res.results]
    return np.concatenate([o.T for o in outs], axis=0)[None]


# version 02
# speedup vs baseline: 1.0208x; 1.0208x over previous
import numpy as np
import ml_dtypes
import concourse.bass as bass
import concourse.mybir as mybir
from concourse.bass_utils import run_bass_kernel_spmd

F32 = mybir.dt.float32
BF16 = mybir.dt.bfloat16
Act = mybir.ActivationFunctionType
Alu = mybir.AluOpType
BFNP = ml_dtypes.bfloat16

C = 2048
TC = 1024
SCT = 512
L = 128
NCHS = 4
NCH = 8
NFT = 16
H = 32
NCORES = 8
KB = 1024
ENGINES = ("sync", "gpsimd", "tensor", "scalar", "vector")


class Sched:

    NDMA = 16

    def __init__(self, nc):
        self.nc = nc
        self.keys = list(ENGINES)
        for e in ("sync", "gpsimd"):
            self.keys += [f"{e}_dma{i}" for i in range(self.NDMA)]
        self.seq = []
        self.count = {k: 0 for k in self.keys}
        self.sems = {k: nc.alloc_semaphore(f"dc_{k}") for k in self.keys}
        self.dmai = {"sync": 0, "gpsimd": 0}

    def job(self, eng, fn, waits=(), inc=1, dma=False, cc=False):
        if cc:
            key = f"{eng}_cc{len(self.keys)}"
            self.keys.append(key)
            self.count[key] = 0
            self.sems[key] = self.nc.alloc_semaphore(key)
        elif dma:
            key = f"{eng}_dma{self.dmai[eng] % self.NDMA}"
            self.dmai[eng] += 1
        else:
            key = eng
        w = [(k, v) for (k, v) in waits if v > 0]
        self.seq.append((key, fn, w, inc))
        self.count[key] += inc
        return (key, self.count[key])

    def tickets(self):
        return {k: self.count[k] for k in self.keys}

    def barrier(self, skip_cc=False):
        snap = self.tickets()
        out = {}
        for e in ENGINES:
            def fn(o, sem, inc, _e=e, _v=snap[e]):
                if _v > 0:
                    o.wait_ge(self.sems[_e], _v)
                o.nop().then_inc(sem, inc)
            out[e] = self.job(
                e, fn, waits=[(k, snap[k]) for k in self.keys
                              if k != e and not (skip_cc and "_cc" in k)], inc=1)
        return out

    def emit(self, block):
        for e in ENGINES:
            keys = tuple([e] + [k for k in self.keys
                                if k.startswith(e + "_dma") or
                                k.startswith(e + "_cc")])

            def body(obj, _keys=keys):
                hw = {k: 0 for k in self.keys}
                for (key, fn, waits, inc) in self.seq:
                    if key not in _keys:
                        continue
                    for (wk, wv) in waits:
                        if wk == _keys[0]:
                            continue
                        if wv > hw[wk]:
                            obj.wait_ge(self.sems[wk], wv)
                            hw[wk] = wv
                    fn(obj, self.sems[key], inc)
            getattr(block, e)(body)


def mm_group(tensor, sem, inc, mms):
    last = None
    for m in mms:
        last = m(tensor)
    last.then_inc(sem, inc)


class P:
    pass


def _consts_np():
    one = np.ones((L, L), np.float32)
    tophalf = np.zeros((L, L), np.float32)
    tophalf[:64, :] = 1.0
    colsel = np.zeros((128, 64), np.float32)
    colselh = np.zeros((128, 64), np.float32)
    nsuf = np.zeros((128, 64), np.float32)
    for j in range(8):
        colsel[:, j * 8 + j] = 1.0
        colselh[:64, j * 8 + j] = 1.0
        nsuf[:, j * 8:j * 8 + j] = -1.0
    bh2 = np.zeros((128, 2), np.float32)
    bh2[:64, 0] = 1.0
    bh2[64:, 1] = 1.0
    bc2 = np.zeros((2, 128), np.float32)
    bc2[0, :64] = 1.0
    bc2[1, 64:] = 1.0
    return dict(
        I128F=np.eye(128, dtype=np.float32),
        I128B=np.eye(128, dtype=np.float32).astype(BFNP),
        MK=(np.triu(one) - tophalf).astype(BFNP),
        MR=(np.triu(one, 1) - tophalf).astype(BFNP),
        MKL=(-np.tril(one, -1)).astype(BFNP),
        MASKU=np.triu(one, 1).astype(BFNP),
        COLSEL=colsel.astype(BFNP), COLSELH=colselh.astype(BFNP),
        NSUF=nsuf.astype(BFNP),
        BH2=bh2.astype(BFNP), BC2=bc2,
        ONES1=np.ones((128, 1), np.float32),
        ONESR=np.ones((1, 128), np.float32),
    )


def v3(ap2d, inner):
    return ap2d.rearrange("p (a b) -> p a b", b=inner)


def build(debug=False):
    nc = bass.Bass("TRN2", target_bir_lowering=False, debug=False,
                   detect_race_conditions=False)
    p = P()
    p.nc = nc
    p.debug = debug
    p.s = Sched(nc)

    def ein(name, shape):
        return nc.dram_tensor(name, list(shape), F32, kind="ExternalInput").ap()

    p.xT = ein("xT", (C, TC + 1))
    p.w_l = {k: ein(f"w{k}_l", (16, 128, 16, 128)) for k in ("k", "r", "g", "o")}
    p.wv_r = ein("wv_r", (C, C))
    p.w1_l = ein("w1_l", (16, 128, 160))
    p.w2_l = ein("w2_l", (5, 32, C))
    p.tdw1_l = ein("tdw1_l", (16, 128, 64))
    p.tdw2_r = ein("tdw2_r", (64, C))
    p.tdb = ein("tdb", (128, C))
    p.maa6 = ein("maa6", (128, NFT, 6))
    p.u_h = ein("u_h", (128, NFT))
    p.lnw_h = ein("lnw_h", (128, NFT))
    p.lnb_h = ein("lnb_h", (128, NFT))
    p.pm = ein("pm", (128, 8))
    p.pmi = ein("pmi", (128, 8))
    p.qgu_f32 = ein("qgu_f32", (C, C))
    p.qpr_f32 = ein("qpr_f32", (4 * C, 256))
    p.out = nc.dram_tensor("out", [C, TC], F32, kind="ExternalOutput").ap()

    p.cn = _consts_np()
    p.cni = {k: nc.inline_tensor(v, f"c_{k}") for k, v in p.cn.items()}

    def dram(name, shape, dt, dbg=False):
        kind = "ExternalOutput" if (dbg and debug) else "Internal"
        return nc.dram_tensor(name, list(shape), dt, kind=kind).ap()

    p.spill_k = dram("spill_k", (128, NCH * C), BF16, dbg=True)
    p.spill_v = dram("spill_v", (128, NCH * C), BF16, dbg=True)
    p.spill_e = dram("spill_e", (128, NCH * C), BF16, dbg=True)
    p.spill_x1 = dram("spill_x1", (128, NFT * TC), F32, dbg=True)
    p.qgu_mine = dram("qgu_mine", (16, 128, 16, 128), BF16)
    p.qpr_mine = dram("qpr_mine", (2, 128, 64, 128), BF16)
    p.qgu_all = dram("qgu_all", (8, 16, 128, 16, 128), BF16)
    p.qpr_all = dram("qpr_all", (8, 2, 128, 64, 128), BF16)
    p.msg_in = dram("msg_in", (128, 1040), F32)
    p.msg_all = dram("msg_all", (8, 128, 1040), F32)
    if debug:
        p.dbg_msg = dram("dbg_msg", (8, 128, 1040), F32, dbg=True)
        p.dbg_msgin = dram("dbg_msgin", (128, 1040), F32, dbg=True)
        p.dbg_qgu = dram("dbg_qgu", (16, 128, 16, 128), F32, dbg=True)
        p.dbg_qpr = dram("dbg_qpr", (2, 128, 64, 128), F32, dbg=True)
        p.dbg_clm = dram("dbg_clm", (128, 384), F32, dbg=True)
        p.dbg_D = dram("dbg_D", (128, 128), F32, dbg=True)
        p.dbg_sloc = dram("dbg_sloc", (128, NCH * 1024), F32, dbg=True)
        p.dbg_sct = dram("dbg_sct", (128, NCH * 4096), F32, dbg=True)
        p.dbg_s0 = dram("dbg_s0", (128, NCH * 2048), F32, dbg=True)
        p.dbg_dtm = dram("dbg_dtm", (128, NCH * 32), F32, dbg=True)
        p.dbg_til = dram("dbg_til", (128, NCH * 4096), F32, dbg=True)
        p.dbg_yraw = dram("dbg_yraw", (128, NCH * C), F32, dbg=True)
        p.dbg_ydv = dram("dbg_ydv", (128, NCH * C), F32, dbg=True)
        p.dbg_gst = dram("dbg_gst", (128, NCH * 256), F32, dbg=True)
        p.dbg_y = dram("dbg_y", (128, NCH * C), F32, dbg=True)
        p.dbg_z = dram("dbg_z", (128, NFT * TC), F32, dbg=True)
        p.dbg_srun = dram("dbg_srun", (128, 8 * 1024), F32, dbg=True)

    sb = nc.alloc_sbuf_tensor
    for k in p.cn:
        setattr(p, k, sb("s" + k, list(p.cn[k].shape),
                         BF16 if p.cn[k].dtype == BFNP else F32).ap())
    p.w1_sb = sb("sw1", [128, 16 * 160], BF16).ap()
    p.tdw1_sb = sb("stdw1", [128, 16 * 64], BF16).ap()
    p.tdw2_sb = sb("stdw2", [64, C], BF16).ap()
    p.tdb_sb = sb("stdb", [128, C], F32).ap()
    p.maa_sb = sb("smaa", [128, NFT * 6], F32).ap()
    p.u_sb = sb("su", [128, NFT], F32).ap()
    p.lnw_sb = sb("slnw", [128, NFT], F32).ap()
    p.lnb_sb = sb("slnb", [128, NFT], F32).ap()
    p.pm_sb = sb("spm", [128, 8], F32).ap()
    p.pmi_sb = sb("spmi", [128, 8], F32).ap()
    p.sloc = sb("ssloc", [128, NCH * 1024], BF16).ap()
    p.cLcM = sb("scLcM", [128, 384], F32).ap()
    p.A_ch = sb("sA_ch", [128, 128], F32).ap()
    p.enm = sb("senm", [128, 128], F32).ap()
    p.srun = sb("ssrun", [128, 1024], F32).ap()
    p.epsb = sb("sepsb", [128, 4], F32).ap()

    rem = nc.sbuf_bytes_remaining - 64
    p.arena0, _ = nc.bump_sbuf(rem)
    p.arena_sz = rem
    p._nm = {}

    def at(name, off_kb, shape, dt):
        n = p._nm.get(name, -1) + 1
        p._nm[name] = n
        nm = name if n == 0 else f"{name}_{n}"
        off = int(off_kb * KB)
        off = (off + 31) // 32 * 32
        nbytes = int(np.prod(shape[1:])) * (2 if dt == BF16 else 4)
        assert off + nbytes <= p.arena_sz, f"{nm}: {off}+{nbytes}>{p.arena_sz}"
        return nc.alloc_sbuf_tensor_at(nm, list(shape), dt,
                                       offset=p.arena0 + off).ap()

    p.xbf = at("xbf", 0, (128, NFT * SCT), BF16)
    p.xx = at("xx", 16, (128, NFT * SCT), BF16)
    p.mixA = at("mixA", 32, (128, NFT * SCT), BF16)
    p.mixB = at("mixB", 48, (128, NFT * SCT), BF16)
    p.ksc = at("ksc", 64, (128, NFT * SCT), BF16)
    p.vsc = at("vsc", 80, (128, NCHS * C), BF16)
    p.zsc = at("zsc", 80, (128, NFT * SCT), BF16)
    p.esc = at("esc", 96, (128, NCHS * C), BF16)
    p.gsc = at("gsc", 96, (128, NFT * SCT), BF16)
    p.kch = at("kch", 0, (128, 2 * C), BF16)
    p.vch = at("vch", 8, (128, 2 * C), BF16)
    p.ech = at("ech", 16, (128, 2 * C), BF16)
    p.cmk = at("cmk", 24, (128, C), F32)
    p.cmr = at("cmr", 32, (128, C), F32)
    p.Fk = at("Fk", 40, (128, C), BF16)
    p.Fr = at("Fr", 44, (128, C), BF16)
    p.ktil = at("ktil", 48, (128, C), BF16)
    p.rtil = at("rtil", 52, (128, C), BF16)
    p.scT = at("scT", 56, (128, H * L), BF16)
    p.ysb = at("ysb", 112, (128, C), F32)
    p.khA = at("khA", 120, (128, C), BF16)
    p.khB = at("khB", 124, (128, C), BF16)
    p.S0A = at("S0A", 120, (128, 1024), BF16)
    p.S0B = at("S0B", 124, (128, 1024), BF16)
    p.fkl = at("fkl", 128, (128, C), BF16)
    p.wvsl = at("wvsl", 128, (128, 2 * 512), BF16)
    p.dtm = at("dtm", 128, (128, 32), F32)
    p.gstat = at("gstat", 128.25, (128, 8 * 32), F32)
    p.rku = at("rku", 132, (128, C), BF16)
    p.w2sl = at("w2sl", 132, (32, 2 * C), BF16)
    p.wsl = at("wsl", 140, (128, 2 * C), BF16)
    p.xst = at("xst", 148, (128, 2 * 513), F32)
    p.mpart = at("mpart", 152.01, (32, 5 * SCT), BF16)
    p.t1 = at("t1", 157.02, (64, SCT), BF16)
    p.tmpA = at("tmpA", 158.03, (128, 2 * SCT), F32)
    p.r1 = at("r1", 0, (128, NFT * SCT), F32)
    p.r1sq = at("r1sq", 112, (128, 2 * 512), F32)
    p.rows = at("rows", 116, (1, 1536), F32)
    p.cb_msgs = at("cb_msgs", 0, (128, 8 * 1040), F32)
    p.cb_ae = at("cb_ae", 34, (128, 16), F32)
    p.cb_sm = at("cb_sm", 35, (128, 1024), F32)
    p.cb_t1 = at("cb_t1", 40, (128, 128), F32)
    p.cb_t2 = at("cb_t2", 40.5, (128, 128), F32)
    p.cb_t3 = at("cb_t3", 41, (128, 16), F32)
    p.msg_sb = at("msg_sb", 42, (128, 1040), F32)

    p.f_h = at("f_h", 0, (128, 64 * SCT), BF16)
    p.f_r2 = at("f_r2", 64, (128, NFT * SCT), F32)
    p.f_prsl = at("f_prsl", 96, (128, 2 * 32 * 128), BF16)
    p.f_x1b = at("f_x1b", 112, (128, NFT * SCT), BF16)
    p.f_gusl = at("f_gusl", 128, (128, 2 * C), BF16)
    p.f_x1t = at("f_x1t", 136, (128, 2 * SCT), F32)
    p.f_sg = at("f_sg", 140, (128, SCT), BF16)
    p.f_h2 = at("f_h2", 141, (128, 2 * SCT), F32)
    p.f_rows = at("f_rows", 145, (1, 3 * SCT), F32)
    p.f_rstdb = at("f_rstdb", 151, (128, SCT), F32)

    p.q_w32 = at("q_w32", 0, (128, 2 * C), F32)
    p.q_wbf = at("q_wbf", 16, (128, 2 * C), BF16)
    p.q_abs = at("q_abs", 24, (128, 2 * C), BF16)
    p.q_s = at("q_s", 32, (2, C), BF16)
    p.q_rs = at("q_rs", 36, (2, C), F32)
    p.q_sbc = at("q_sbc", 44, (128, C), F32)
    p.q_r = at("q_r", 52, (128, C), BF16)
    p.q_a = at("q_a", 56, (128, C), BF16)
    p.q_b = at("q_b", 60, (128, C), BF16)
    p.q_q = at("q_q", 64, (128, C), BF16)
    p.q_wh = at("q_wh", 68, (128, C), BF16)
    p.q_t = at("q_t", 72, (128, C), BF16)
    p.q_out = at("q_out", 76, (128, 2 * C), BF16)
    p.q_sbc2 = at("q_sbc2", 84, (128, C), F32)
    p.q_s32 = at("q_s32", 92, (2, C), F32)

    p.PBg = nc.psum_tensor("PBIG", [128, 4096], F32)
    p.PB = p.PBg.__enter__().ap()
    return p


def pb(p, bank, rows=128, cols=512, off=0, dt=F32):
    v = p.PB[0:rows, bank * 512 + off:bank * 512 + off + cols]
    if dt == BF16:
        v = v.bitcast(BF16)
    return v


def ph_consts(p):
    S = p.s

    def ld_sync(o, sem, inc):
        pairs = [
            (p.I128F, "I128F"), (p.I128B, "I128B"), (p.MK, "MK"), (p.MR, "MR"),
            (p.MKL, "MKL"), (p.MASKU, "MASKU"), (p.COLSEL, "COLSEL"),
            (p.COLSELH, "COLSELH"), (p.NSUF, "NSUF"),
            (p.BH2, "BH2"), (p.BC2, "BC2"),
            (p.ONES1, "ONES1"), (p.ONESR, "ONESR"),
        ]
        for dst, k in pairs:
            o.dma_start(out=dst, in_=p.cni[k].ap()).then_inc(sem, 16)
        for dst, src in [(p.tdb_sb, p.tdb), (v3(p.maa_sb, 6), p.maa6),
                         (p.u_sb, p.u_h), (p.lnw_sb, p.lnw_h),
                         (p.lnb_sb, p.lnb_h), (p.pm_sb, p.pm),
                         (p.pmi_sb, p.pmi)]:
            o.dma_start(out=dst, in_=src).then_inc(sem, 16)
    S.job("sync", ld_sync, inc=16 * 20, dma=True)

    def ld_gp(o, sem, inc):
        for dst, src in [(p.tdw2_sb, p.tdw2_r),
                         (v3(p.w1_sb, 160), p.w1_l.transpose([1, 0, 2])),
                         (v3(p.tdw1_sb, 64), p.tdw1_l.transpose([1, 0, 2]))]:
            o.dma_start(out=dst, in_=src).then_inc(sem, 16)
    S.job("gpsimd", ld_gp, inc=16 * 3, dma=True)

    def ld_ms(o, sem, inc):
        o.memset(p.khA[:, :], 0.0)
        o.memset(p.khB[:, :], 0.0)
        o.memset(p.epsb[:, 0:1], 1e-5 * 64.0)
        o.memset(p.epsb[:, 1:2], 1e-5)
        o.memset(p.epsb[:, 2:3], 1.1920929e-07)
        o.memset(p.srun[:, :], 0.0).then_inc(sem, inc)
    S.job("gpsimd", ld_ms, inc=1)
    S.barrier(skip_cc=True)


def ph_quant(p):
    S = p.s
    pq_sum = pb(p, 0, rows=2, cols=2048)
    pq_bc = pb(p, 4)
    pq_bc2 = pb(p, 5)

    tiles = [("gu", kt, C) for kt in range(16)] + \
            [("pr", kt, 256) for kt in range(64)]
    t_dma = {}
    t_abs = {}
    t_sum = {}
    t_sc = {}
    t_out = {}
    t_od = {}
    t_bc = {}
    t_sbc = {}

    for i, (kind, kt, w) in enumerate(tiles):
        b = (i % 2) * C
        src = (p.qgu_f32 if kind == "gu" else p.qpr_f32)[kt * 128:(kt + 1) * 128, :]
        nsl = max(1, w // 512)

        def dj(o, sem, inc, src=src, b=b, w=w):
            o.dma_start(out=p.q_w32[:, b:b + w], in_=src).then_inc(sem, inc)
        t_dma[i] = S.job("sync", dj, waits=[t_abs.get(i - 2, ("sync", 0))], inc=16, dma=True)

        def aj(o, sem, inc, b=b, w=w):
            o.copy(out=p.q_wbf[:, b:b + w], in_=p.q_w32[:, b:b + w])
            o.activation(out=p.q_abs[:, b:b + w], in_=p.q_wbf[:, b:b + w],
                         func=Act.Abs).then_inc(sem, inc)
        t_abs[i] = S.job("scalar", aj,
                         waits=[t_dma[i], t_out.get(i - 2, ("sync", 0))], inc=1)

        def sj(o, sem, inc, b=b, w=w, nsl=nsl):
            last = None
            for sl in range(nsl):
                sw = min(w, 512)
                last = o.matmul(pq_sum[:, sl * 512:sl * 512 + sw], p.BH2,
                                p.q_abs[:, b + sl * 512:b + sl * 512 + sw],
                                start=True, stop=True)
            last.then_inc(sem, inc)
        t_sum[i] = S.job("tensor", sj,
                         waits=[t_abs[i], t_sc.get(i - 1, ("sync", 0))], inc=1)

        def scj(o, sem, inc, w=w):
            o.tensor_scalar(p.q_s[:, 0:w], pq_sum[:, 0:w], 1.0 / 64.0, None, Alu.mult)
            o.tensor_scalar_max(p.q_s[:, 0:w], p.q_s[:, 0:w], 1e-8)
            o.tensor_copy(out=p.q_s32[:, 0:w], in_=p.q_s[:, 0:w])
            o.reciprocal(p.q_rs[:, 0:w], p.q_s[:, 0:w]).then_inc(sem, inc)
        t_sc[i] = S.job("vector", scj, waits=[t_sum[i]], inc=1)

        for sl in range(nsl):
            sw = min(w, 512)

            def bj(o, sem, inc, sl=sl, sw=sw):
                o.matmul(pq_bc[:, 0:sw], p.BC2, p.q_rs[:, sl * 512:sl * 512 + sw],
                         start=True, stop=True)
                o.matmul(pq_bc2[:, 0:sw], p.BC2, p.q_s32[:, sl * 512:sl * 512 + sw],
                         start=True, stop=True).then_inc(sem, inc)
            t_bc[(i, sl)] = S.job(
                "tensor", bj,
                waits=[t_sc[i], t_sbc.get((i, sl - 1), t_sbc.get((i - 1, 3),
                       t_sbc.get((i - 1, 0), ("sync", 0))))], inc=1)

            def sbj(o, sem, inc, sl=sl, sw=sw):
                o.tensor_copy(out=p.q_sbc[:, sl * 512:sl * 512 + sw],
                              in_=pq_bc[:, 0:sw])
                o.tensor_copy(out=p.q_sbc2[:, sl * 512:sl * 512 + sw],
                              in_=pq_bc2[:, 0:sw]).then_inc(sem, inc)
            t_sbc[(i, sl)] = S.job("vector", sbj, waits=[t_bc[(i, sl)]], inc=1)

        def ej(o, sem, inc, b=b, w=w):
            o.tensor_mul(p.q_r[:, 0:w], p.q_wbf[:, b:b + w], p.q_sbc[:, 0:w])
            o.tensor_single_scalar(p.q_a[:, 0:w], p.q_r[:, 0:w], 0.5, Alu.is_gt)
            o.tensor_single_scalar(p.q_b[:, 0:w], p.q_r[:, 0:w], -0.5, Alu.is_lt)
            o.tensor_sub(p.q_q[:, 0:w], p.q_a[:, 0:w], p.q_b[:, 0:w])
            o.tensor_mul(p.q_wh[:, 0:w], p.q_q[:, 0:w], p.q_sbc2[:, 0:w])
            o.tensor_sub(p.q_t[:, 0:w], p.q_wh[:, 0:w], p.q_wbf[:, b:b + w])
            o.tensor_add(p.q_out[:, b:b + w], p.q_wbf[:, b:b + w],
                         p.q_t[:, 0:w]).then_inc(sem, inc)
        t_out[i] = S.job("vector", ej,
                         waits=[t_sbc[(i, nsl - 1)],
                                t_od.get(i - 2, ("sync", 0))], inc=1)

        def oj(o, sem, inc, kind=kind, kt=kt, b=b, w=w):
            dst = (p.qgu_mine if kind == "gu" else p.qpr_mine)[:, :, kt, :] \
                .transpose([1, 0, 2])
            o.dma_start(out=dst, in_=v3(p.q_out[:, b:b + w], 128)).then_inc(sem, inc)
        t_od[i] = S.job("gpsimd", oj, waits=[t_out[i]], inc=16, dma=True)

    def agj(o, sem, inc):
        o.collective_compute(
            "AllGather", Alu.bypass, replica_groups=[list(range(NCORES))],
            ins=[p.qgu_mine.opt()], outs=[p.qgu_all.opt()]).then_inc(sem, 1)
        o.collective_compute(
            "AllGather", Alu.bypass, replica_groups=[list(range(NCORES))],
            ins=[p.qpr_mine.opt()], outs=[p.qpr_all.opt()]).then_inc(sem, 1)
    p.t_qag = S.job("gpsimd", agj, waits=[t_od[len(tiles) - 1]], inc=2, cc=True)
    S.barrier(skip_cc=True)


def st_xload(p, sc):
    S = p.s
    t_x = {}
    t_c = {}
    for ft in range(NFT):
        o0 = (ft % 2) * 513

        def dj(o, sem, inc, ft=ft, o0=o0):
            o.dma_start(out=p.xst[:, o0:o0 + 513],
                        in_=p.xT[ft * 128:(ft + 1) * 128, sc * SCT:sc * SCT + 513]) \
                .then_inc(sem, inc)
        t_x[ft] = S.job("sync", dj, waits=[t_c.get((ft - 2, "v"), ("sync", 0)),
                                           t_c.get((ft - 2, "s"), ("sync", 0))],
                        inc=16, dma=True)

        def vj(o, sem, inc, ft=ft, o0=o0):
            o.tensor_sub(p.xx[:, ft * SCT:(ft + 1) * SCT],
                         p.xst[:, o0:o0 + SCT],
                         p.xst[:, o0 + 1:o0 + 513]).then_inc(sem, inc)
        t_c[(ft, "v")] = S.job("vector", vj, waits=[t_x[ft]], inc=1)

        def sj(o, sem, inc, ft=ft, o0=o0):
            o.copy(out=p.xbf[:, ft * SCT:(ft + 1) * SCT],
                   in_=p.xst[:, o0 + 1:o0 + 513]).then_inc(sem, inc)
        t_c[(ft, "s")] = S.job("scalar", sj, waits=[t_x[ft]], inc=1)
    return (t_c[(NFT - 1, "v")], t_c[(NFT - 1, "s")])


def st_mstage(p, fsel, xdone):
    S = p.s
    pm = {f: pb(p, i, rows=32) for i, f in enumerate(fsel)}
    t_xxx = {}
    for ft in range(NFT):
        def vj(o, sem, inc, ft=ft):
            o.scalar_tensor_tensor(
                p.mixA[:, ft * SCT:(ft + 1) * SCT],
                p.xx[:, ft * SCT:(ft + 1) * SCT],
                p.maa_sb[:, ft * 6:ft * 6 + 1],
                p.xbf[:, ft * SCT:(ft + 1) * SCT],
                Alu.mult, Alu.add).then_inc(sem, inc)
        t_xxx[ft] = S.job("vector", vj, waits=list(xdone), inc=1)

    t_mm = {}
    for f in fsel:
        def tj(o, sem, inc, f=f):
            last = None
            for kt in range(NFT):
                last = o.matmul(pm[f][:, :],
                                p.w1_sb[:, kt * 160 + f * 32:kt * 160 + (f + 1) * 32],
                                p.mixA[:, kt * SCT:(kt + 1) * SCT],
                                start=(kt == 0), stop=(kt == NFT - 1))
            last.then_inc(sem, inc)
        t_mm[f] = S.job("tensor", tj, waits=[t_xxx[NFT - 1]], inc=1)

    t_mt = {}
    for f in fsel:
        def sj(o, sem, inc, f=f):
            o.activation(out=p.mpart[0:32, f * SCT:(f + 1) * SCT],
                         in_=pm[f][:, :], func=Act.Tanh).then_inc(sem, inc)
        t_mt[f] = S.job("scalar", sj, waits=[t_mm[f]], inc=1)
    return t_mt


def st_mix(p, f, dest, t_m, side, extra=None, fence=()):
    S = p.s
    pmx = [pb(p, 3), pb(p, 4)]
    w2s = p.w2sl[0:32, side * C:(side + 1) * C]

    def wj(o, sem, inc):
        o.dma_start(out=w2s, in_=p.w2_l[f]).then_inc(sem, inc)
    t_w2 = S.job("gpsimd", wj, waits=([] if extra is None else [extra]), inc=16, dma=True)

    t_mx = {}
    t_ta = {}
    t_mix = {}
    for mt in range(NFT):
        def tj(o, sem, inc, mt=mt):
            o.matmul(pmx[mt % 2][:, :], w2s[:, mt * 128:(mt + 1) * 128],
                     p.mpart[0:32, f * SCT:(f + 1) * SCT],
                     start=True, stop=True).then_inc(sem, inc)
        t_mx[mt] = S.job("tensor", tj,
                         waits=[t_w2, t_m, t_ta.get(mt - 2, ("sync", 0))] +
                               ([] if extra is None else [extra]) +
                               (list(fence) if mt == 0 else []), inc=1)

        def vj(o, sem, inc, mt=mt):
            o.scalar_tensor_tensor(
                p.tmpA[:, (mt % 2) * SCT:(mt % 2 + 1) * SCT],
                pmx[mt % 2][:, :],
                p.maa_sb[:, mt * 6 + f + 1:mt * 6 + f + 2],
                p.xx[:, mt * SCT:(mt + 1) * SCT],
                Alu.add, Alu.mult).then_inc(sem, inc)
        t_ta[mt] = S.job("vector", vj,
                         waits=[t_mx[mt], t_mix.get(mt - 2, ("sync", 0))], inc=1)

        def gj(o, sem, inc, mt=mt):
            o.tensor_add(dest[:, mt * SCT:(mt + 1) * SCT],
                         p.tmpA[:, (mt % 2) * SCT:(mt % 2 + 1) * SCT],
                         p.xbf[:, mt * SCT:(mt + 1) * SCT]).then_inc(sem, inc)
        t_mix[mt] = S.job("gpsimd", gj, waits=[t_ta[mt]], inc=1)
    return t_mix[NFT - 1], t_ta[NFT - 1]


def st_proj_fm(p, wkey, src, t_src, evac, eng_ev="vector", fence=()):
    S = p.s
    pp = [pb(p, 5), pb(p, 6)]
    t_w = {}
    t_k = {}
    t_e = {}
    for mt in range(NFT):
        def wj(o, sem, inc, mt=mt):
            o.dma_start(out=v3(p.wsl[:, (mt % 2) * C:(mt % 2 + 1) * C], 128),
                        in_=p.w_l[wkey][mt]).then_inc(sem, inc)
        t_w[mt] = S.job("gpsimd", wj, waits=[t_k.get(mt - 2, ("sync", 0))], inc=16, dma=True)

        def tj(o, sem, inc, mt=mt):
            last = None
            for kt in range(NFT):
                last = o.matmul(
                    pp[mt % 2][:, :],
                    p.wsl[:, (mt % 2) * C + kt * 128:(mt % 2) * C + (kt + 1) * 128],
                    src[:, kt * SCT:(kt + 1) * SCT],
                    start=(kt == 0), stop=(kt == NFT - 1))
            last.then_inc(sem, inc)
        t_k[mt] = S.job("tensor", tj,
                        waits=[t_w[mt], t_src, t_e.get(mt - 2, ("sync", 0))] +
                              (list(fence) if mt == 0 else []), inc=1)

        def ej(o, sem, inc, mt=mt):
            evac(o, mt, pp[mt % 2][:, :], sem, inc)
        t_e[mt] = S.job(eng_ev, ej, waits=[t_k[mt]], inc=1)
    return t_e


def st_proj_v(p, src, t_src, fence=()):
    S = p.s
    pv = [pb(p, 0), pb(p, 1), pb(p, 2), pb(p, 5)]
    t_w = {}
    t_k = {}
    t_e = {}
    for n_co in range(4):
        for kt in range(NFT):
            i = n_co * NFT + kt

            def wj(o, sem, inc, n_co=n_co, kt=kt, i=i):
                o.dma_start(out=p.wvsl[:, (i % 2) * 512:(i % 2 + 1) * 512],
                            in_=p.wv_r[kt * 128:(kt + 1) * 128,
                                       n_co * 512:(n_co + 1) * 512]).then_inc(sem, inc)
            t_w[i] = S.job("gpsimd", wj, waits=[t_k.get(i - 2, ("sync", 0))], inc=16, dma=True)

            def tj(o, sem, inc, n_co=n_co, kt=kt, i=i):
                last = None
                for cj in range(NCHS):
                    last = o.matmul(
                        pv[cj][:, :],
                        src[:, kt * SCT + cj * 128:kt * SCT + (cj + 1) * 128],
                        p.wvsl[:, (i % 2) * 512:(i % 2 + 1) * 512],
                        start=(kt == 0), stop=(kt == NFT - 1))
                last.then_inc(sem, inc)
            t_k[i] = S.job("tensor", tj,
                           waits=[t_w[i], t_src,
                                  t_e.get(n_co - 1, ("sync", 0))] +
                                 (list(fence) if i == 0 else []), inc=1)

        def ej(o, sem, inc, n_co=n_co):
            for cj in range(NCHS):
                ii = 1 if cj == NCHS - 1 else 1
                inst = o.tensor_copy(
                    out=p.vsc[:, cj * C + n_co * 512:cj * C + (n_co + 1) * 512],
                    in_=pv[cj][:, :])
                inst.then_inc(sem, 1)
        t_e[n_co] = S.job("vector", ej, waits=[t_k[n_co * NFT + NFT - 1]], inc=4)
    return t_e[3]


def st_wpath(p, t_src, fence=()):
    S = p.s
    pw1 = pb(p, 6, rows=64)
    pwt = [pb(p, 3), pb(p, 4)]

    def t1j(o, sem, inc):
        last = None
        for kt in range(NFT):
            last = o.matmul(pw1[:, :], p.tdw1_sb[:, kt * 64:(kt + 1) * 64],
                            p.mixB[:, kt * SCT:(kt + 1) * SCT],
                            start=(kt == 0), stop=(kt == NFT - 1))
        last.then_inc(sem, inc)
    t_s1 = S.job("tensor", t1j, waits=[t_src] + list(fence), inc=1)

    def s1j(o, sem, inc):
        o.activation(out=p.t1[0:64, :], in_=pw1[:, :],
                     func=Act.Tanh).then_inc(sem, inc)
    t_t1 = S.job("scalar", s1j, waits=[t_s1], inc=1)

    t_m = {}
    t_a = {}
    t_ee = {}
    for i in range(16):
        cj, n_co = i // 4, i % 4

        def tj(o, sem, inc, cj=cj, n_co=n_co, i=i):
            o.matmul(pwt[i % 2][:, :], p.t1[0:64, cj * 128:(cj + 1) * 128],
                     p.tdw2_sb[:, n_co * 512:(n_co + 1) * 512],
                     start=True, stop=True).then_inc(sem, inc)
        t_m[i] = S.job("tensor", tj, waits=[t_t1, t_a.get(i - 2, ("sync", 0))],
                       inc=1)

        def vj(o, sem, inc, i=i, n_co=n_co):
            o.tensor_add(p.tmpA[:, (i % 2) * 512:(i % 2 + 1) * 512],
                         pwt[i % 2][:, :],
                         p.tdb_sb[:, n_co * 512:(n_co + 1) * 512]).then_inc(sem, inc)
        t_a[i] = S.job("vector", vj, waits=[t_m[i], t_ee.get(i - 2, ("sync", 0))],
                       inc=1)

        def sj(o, sem, inc, i=i, cj=cj, n_co=n_co):
            o.activation(
                out=p.esc[:, cj * C + n_co * 512:cj * C + (n_co + 1) * 512],
                in_=p.tmpA[:, (i % 2) * 512:(i % 2 + 1) * 512],
                func=Act.Exp).then_inc(sem, inc)
        t_ee[i] = S.job("scalar", sj, waits=[t_a[i]], inc=1)
    return t_ee[15]


def st_summary(p, sc, pcl, t_e, t_k, t_v):
    S = p.s
    paux = [pb(p, 0), pb(p, 1)]
    ptr = pb(p, 2, dt=BF16)
    pS = pb(p, 5, cols=1024)

    t_sle_prev = ("sync", 0)
    for cj in range(NCHS):
        j = sc * NCHS + cj
        co = cj * C
        t_kl = {}
        t_kle = {}
        for n in range(4):
            def tj(o, sem, inc, n=n, co=co):
                o.matmul(paux[n % 2][:, :], p.MKL,
                         p.esc[:, co + n * 512:co + (n + 1) * 512],
                         start=True, stop=True).then_inc(sem, inc)
            t_kl[n] = S.job("tensor", tj,
                            waits=[t_e, t_kle.get(n - 2, ("sync", 0))], inc=1)

            def sj(o, sem, inc, n=n):
                o.activation(out=p.fkl[:, n * 512:(n + 1) * 512],
                             in_=paux[n % 2][:, :], func=Act.Exp).then_inc(sem, inc)
            t_kle[n] = S.job("scalar", sj, waits=[t_kl[n]], inc=1)

        first = (sc == 0 and cj == 0)
        last = (sc == 1 and cj == 3)

        def cj_(o, sem, inc, j=j, co=co, first=first, last=last):
            for ft in range(NFT):
                o.matmul(pcl[:, ft * 8:(ft + 1) * 8],
                         p.esc[:, co + ft * 128:co + (ft + 1) * 128],
                         p.COLSEL[:, j * 8:(j + 1) * 8],
                         start=(first and ft == 0), stop=False)
            for ft in range(NFT):
                o.matmul(pcl[:, 128 + ft * 8:128 + (ft + 1) * 8],
                         p.esc[:, co + ft * 128:co + (ft + 1) * 128],
                         p.COLSELH[:, j * 8:(j + 1) * 8],
                         start=False, stop=False)
            for ft in range(NFT):
                inst = o.matmul(pcl[:, 256 + ft * 8:256 + (ft + 1) * 8],
                                p.esc[:, co + ft * 128:co + (ft + 1) * 128],
                                p.NSUF[:, j * 8:(j + 1) * 8],
                                start=False, stop=(last and ft == NFT - 1))
            inst.then_inc(sem, inc)
        t_cl = S.job("tensor", cj_, waits=[t_e], inc=1)
        if last:
            p.t_clast = t_cl

        t_tr = {}
        t_kh = {}
        for g in range(2):
            def trj(o, sem, inc, g=g, cj=cj):
                last_i = None
                for fl in range(8):
                    ft = g * 8 + fl
                    last_i = o.transpose(
                        ptr[:, fl * 128:(fl + 1) * 128],
                        p.ksc[:, ft * SCT + cj * 128:ft * SCT + (cj + 1) * 128],
                        p.I128B)
                last_i.then_inc(sem, inc)
            t_tr[g] = S.job("tensor", trj,
                            waits=[t_k, t_kh.get(g - 1, ("sync", 0))], inc=1)

            def khj(o, sem, inc, g=g):
                for half, buf in ((0, p.khA), (1, p.khB)):
                    inst = o.tensor_mul(
                        v3(buf, 128)[:, g * 8:(g + 1) * 8, half * 64:half * 64 + 64],
                        v3(ptr, 128)[:, 0:8, half * 64:half * 64 + 64],
                        v3(p.fkl, 128)[:, g * 8:(g + 1) * 8,
                                       half * 64:half * 64 + 64])
                inst.then_inc(sem, inc)
            t_kh[g] = S.job("vector", khj, waits=[t_tr[g], t_kle[3]], inc=1)

        def slj(o, sem, inc, co=co):
            for ft in range(NFT):
                o.matmul(pS[:, ft * 64:(ft + 1) * 64],
                         p.khA[:, ft * 128:(ft + 1) * 128],
                         p.vsc[:, co + ft * 128:co + ft * 128 + 64],
                         start=(ft % 8 == 0), stop=False)
                inst = o.matmul(pS[:, ft * 64:(ft + 1) * 64],
                                p.khB[:, ft * 128:(ft + 1) * 128],
                                p.vsc[:, co + ft * 128 + 64:co + (ft + 1) * 128],
                                start=False, stop=(ft % 8 == 7))
            inst.then_inc(sem, inc)
        t_sl = S.job("tensor", slj, waits=[t_kh[1], t_v, t_sle_prev], inc=1)

        def slev(o, sem, inc, j=j):
            o.tensor_copy(out=p.sloc[:, j * 1024:(j + 1) * 1024],
                          in_=pS[:, :]).then_inc(sem, inc)
        t_sle_prev = S.job("vector", slev, waits=[t_sl], inc=1)

    def spj(o, sem, inc):
        for cj in range(NCHS):
            j = sc * NCHS + cj
            o.dma_start(out=v3(p.spill_k[:, j * C:(j + 1) * C], 128),
                        in_=v3(p.ksc, SCT)[:, :, cj * 128:(cj + 1) * 128]) \
                .then_inc(sem, 16)
            o.dma_start(out=p.spill_v[:, j * C:(j + 1) * C],
                        in_=p.vsc[:, cj * C:(cj + 1) * C]).then_inc(sem, 16)
            o.dma_start(out=p.spill_e[:, j * C:(j + 1) * C],
                        in_=p.esc[:, cj * C:(cj + 1) * C]).then_inc(sem, 16)
    t_sp = S.job("sync", spj, waits=[t_e, t_k, t_v], inc=16 * 12, dma=True)
    return t_sle_prev, t_sp


def ph_pass1_sc(p, sc, pcl):
    S = p.s
    xdone = st_xload(p, sc)
    t_m = st_mstage(p, [0, 1, 2], xdone)
    t_mixk, t_ta_k = st_mix(p, 1, p.mixB, t_m[1], 0)

    def evk(o, mt, ps, sem, inc):
        o.tensor_copy(out=p.ksc[:, mt * SCT:(mt + 1) * SCT],
                      in_=ps).then_inc(sem, inc)
    t_ke = st_proj_fm(p, "k", p.mixB, t_mixk, evk)
    t_mixv, t_ta_v = st_mix(p, 2, p.mixA, t_m[2], 1, fence=[t_ta_k])
    t_ve = st_proj_v(p, p.mixA, t_mixv,
                     fence=[t_m[0], t_m[1], t_m[2], t_ke[NFT - 1]])
    t_mixw, t_ta_w = st_mix(p, 0, p.mixB, t_m[0], 0, extra=t_ke[NFT - 1],
                            fence=[t_ta_v])
    t_ee = st_wpath(p, t_mixw, fence=[t_ke[NFT - 1]])
    t_sle, t_sp = st_summary(p, sc, pcl, t_ee, t_ke[NFT - 1], t_ve)
    return t_sle, t_sp


def ph_msg(p):
    S = p.s

    def evj(o, sem, inc):
        o.tensor_copy(out=p.cLcM[:, :], in_=p.pcl_ap[:, :]).then_inc(sem, inc)
    t_cle = S.job("vector", evj, waits=[p.t_clast], inc=1)

    def prepj(o, sem, inc):
        o.reduce_sum(p.cb_t3[:, :], v3(p.cLcM[:, 0:128], 8),
                     axis=mybir.AxisListType.X).then_inc(sem, inc)
    t_prep = S.job("vector", prepj, waits=[t_cle], inc=1)

    def expj(o, sem, inc):
        o.activation(out=p.cb_t1[:, :], in_=p.cLcM[:, 256:384], func=Act.Exp)
        o.activation(out=p.msg_sb[:, 0:16], in_=p.cb_t3[:, :],
                     func=Act.Exp, scale=-1.0)
        o.activation(out=p.A_ch[:, :], in_=p.cLcM[:, 0:128],
                     func=Act.Exp, scale=-1.0)
        o.activation(out=p.enm[:, :], in_=p.cLcM[:, 128:256],
                     func=Act.Exp, scale=-1.0).then_inc(sem, inc)
    t_exp = S.job("scalar", expj, waits=[t_prep], inc=1)

    def smj(o, sem, inc):
        o.memset(p.msg_sb[:, 16:1040], 0.0)
        for j in range(NCH):
            for ft in range(NFT):
                inst = o.scalar_tensor_tensor(
                    p.msg_sb[:, 16 + ft * 64:16 + (ft + 1) * 64],
                    p.sloc[:, j * 1024 + ft * 64:j * 1024 + (ft + 1) * 64],
                    p.cb_t1[:, ft * 8 + j:ft * 8 + j + 1],
                    p.msg_sb[:, 16 + ft * 64:16 + (ft + 1) * 64],
                    Alu.mult, Alu.add)
        inst.then_inc(sem, inc)
    t_sm = S.job("vector", smj, waits=[t_exp], inc=1)

    def dmaj(o, sem, inc):
        o.dma_start(out=p.msg_in, in_=p.msg_sb[:, :]).then_inc(sem, 16)
        if p.debug:
            o.dma_start(out=p.dbg_msgin, in_=p.msg_sb[:, :]).then_inc(sem, 16)
            o.dma_start(out=p.dbg_clm, in_=p.cLcM[:, :]).then_inc(sem, 16)
            o.dma_start(out=p.dbg_D, in_=p.cb_t1[:, :]).then_inc(sem, 16)
    t_dma = S.job("sync", dmaj, waits=[t_sm],
                  inc=16 * (4 if p.debug else 1), dma=True)

    if getattr(p, "no_cc", False):
        def fakeag(o, sem, inc):
            for cc in range(NCORES):
                o.dma_start(out=p.msg_all[cc], in_=p.msg_sb[:, :]).then_inc(sem, 16)
        p.t_mag = S.job("sync", fakeag, waits=[t_sm], inc=16 * 8, dma=True)
        return

    def agj(o, sem, inc):
        o.collective_compute(
            "AllGather", Alu.bypass, replica_groups=[list(range(NCORES))],
            ins=[p.msg_in.opt()], outs=[p.msg_all.opt()]).then_inc(sem, inc)
    p.t_mag = S.job("gpsimd", agj, waits=[t_dma], inc=1, cc=True)


def ph_combine(p):
    S = p.s

    def lj(o, sem, inc):
        for cc in range(NCORES):
            o.dma_start(out=p.cb_msgs[:, cc * 1040:(cc + 1) * 1040],
                        in_=p.msg_all[cc]).then_inc(sem, 16)
        if p.debug:
            for cc in range(NCORES):
                o.dma_start(out=p.dbg_msg[cc],
                            in_=p.cb_msgs[:, cc * 1040:(cc + 1) * 1040]) \
                    .then_inc(sem, 16)
    t_l = S.job("sync", lj, waits=[p.t_mag], inc=16 * (8 + (8 if p.debug else 0)),
                dma=True)

    def zj(o, sem, inc):
        o.memset(p.S0A[:, :], 0.0)
        o.memset(p.S0B[:, :], 0.0).then_inc(sem, inc)
    t_z = S.job("gpsimd", zj, inc=1)

    def cj(o, sem, inc):
        for cc in range(NCORES - 1):
            o.tensor_scalar(p.cb_ae[:, :], p.cb_msgs[:, cc * 1040:cc * 1040 + 16],
                            p.pm_sb[:, cc:cc + 1], p.pmi_sb[:, cc:cc + 1],
                            Alu.mult, Alu.add)
            o.tensor_scalar_mul(p.cb_sm[:, :],
                                p.cb_msgs[:, cc * 1040 + 16:(cc + 1) * 1040],
                                p.pm_sb[:, cc:cc + 1])
            for ft in range(NFT):
                inst = o.scalar_tensor_tensor(
                    p.srun[:, ft * 64:(ft + 1) * 64],
                    p.srun[:, ft * 64:(ft + 1) * 64],
                    p.cb_ae[:, ft:ft + 1],
                    p.cb_sm[:, ft * 64:(ft + 1) * 64],
                    Alu.mult, Alu.add)
        inst.then_inc(sem, inc)
    t_c = S.job("vector", cj, waits=[t_l], inc=1)
    S.barrier(skip_cc=True)
    return t_c, t_z


def st_scan(p, sc, t_r, t_g):
    S = p.s
    py = pb(p, 0, cols=2048)
    psE = pb(p, 4)
    psO = pb(p, 5)
    pa = [pb(p, 6), pb(p, 7)]

    t_ld = {}
    t_til = {}
    t_ye_prev = ("sync", 0)
    t_z_prev = ("sync", 0)
    t_sr_prev = ("sync", 0)
    t_ym3_prev = ("sync", 0)
    t_dbg_prev = ("sync", 0)
    t_dbgall_prev = []
    for cj in range(NCHS):
        j = sc * NCHS + cj
        co = (cj % 2) * C
        snap = S.tickets()
        cfence_t = [("vector", snap["vector"]), ("scalar", snap["scalar"])]
        cfence_v = [("tensor", snap["tensor"]), ("scalar", snap["scalar"])]
        cfence_s = [("tensor", snap["tensor"]), ("vector", snap["vector"])]

        def ldj(o, sem, inc, cj=cj, j=j, co=co):
            for spill, buf in ((p.spill_k, p.kch), (p.spill_v, p.vch),
                               (p.spill_e, p.ech)):
                o.dma_start(out=buf[:, co:co + C],
                            in_=spill[:, j * C:(j + 1) * C]).then_inc(sem, 16)
        t_ld[cj] = S.job("sync", ldj,
                         waits=[t_til.get(cj - 2, ("sync", 0)),
                                t_ye_prev if cj >= 2 else ("sync", 0),
                                t_r, t_g],
                         inc=48, dma=True)

        t_cm = {}
        t_cme = {}
        for mtx, (mat, dst) in enumerate(((p.MK, p.cmk), (p.MR, p.cmr))):
            for n in range(4):
                i = mtx * 4 + n

                def tj(o, sem, inc, mat=mat, n=n, co=co, i=i):
                    o.matmul(pa[i % 2][:, :], mat,
                             p.ech[:, co + n * 512:co + (n + 1) * 512],
                             start=True, stop=True).then_inc(sem, inc)
                t_cm[i] = S.job("tensor", tj,
                                waits=[t_ld[cj], t_cme.get(i - 2, ("sync", 0)),
                                       t_z_prev] + (cfence_t if i == 0 else []),
                                inc=1)

                def ej(o, sem, inc, dst=dst, n=n, i=i):
                    o.tensor_copy(out=dst[:, n * 512:(n + 1) * 512],
                                  in_=pa[i % 2][:, :]).then_inc(sem, inc)
                t_cme[i] = S.job("vector", ej,
                                 waits=[t_cm[i]] +
                                 ((t_dbgall_prev + cfence_v) if i == 0 else []),
                                 inc=1)

        t_fe = {}
        t_tr = {}
        for g in range(8):
            src = p.cmk if g < 4 else p.cmr
            gg = g % 4

            def trj(o, sem, inc, src=src, gg=gg, g=g):
                last = None
                for fl in range(4):
                    last = o.transpose(
                        pa[g % 2][:, fl * 128:(fl + 1) * 128],
                        src[:, (gg * 4 + fl) * 128:(gg * 4 + fl + 1) * 128],
                        p.I128F)
                last.then_inc(sem, inc)
            t_tr[g] = S.job("tensor", trj,
                            waits=[t_cme[7], t_fe.get(g - 2, ("sync", 0))], inc=1)

            def fej(o, sem, inc, g=g, gg=gg):
                dst = p.Fk if g < 4 else p.Fr
                o.activation(out=dst[:, gg * 512:(gg + 1) * 512],
                             in_=pa[g % 2][:, :], func=Act.Exp,
                             scale=(1.0 if g < 4 else -1.0)).then_inc(sem, inc)
            t_fe[g] = S.job("scalar", fej,
                            waits=[t_tr[g]] + (cfence_s if g == 0 else []), inc=1)

        def tilj(o, sem, inc, cj=cj, co=co):
            o.tensor_mul(p.ktil[:, :], p.kch[:, co:co + C], p.Fk[:, :])
            o.tensor_mul(v3(p.rtil, 128)[:, :, :],
                         v3(p.ksc, SCT)[:, :, cj * 128:(cj + 1) * 128],
                         v3(p.Fr, 128)[:, :, :])
            o.tensor_mul(v3(p.rku, 128)[:, :, :],
                         v3(p.ksc, SCT)[:, :, cj * 128:(cj + 1) * 128],
                         v3(p.kch[:, co:co + C], 128)[:, :, :])
            for ft in range(NFT):
                inst = o.tensor_scalar_mul(p.rku[:, ft * 128:(ft + 1) * 128],
                                           p.rku[:, ft * 128:(ft + 1) * 128],
                                           p.u_sb[:, ft:ft + 1])
            inst.then_inc(sem, inc)
        t_til[cj] = S.job("vector", tilj, waits=[t_fe[7], t_r], inc=1)

        def dj(o, sem, inc):
            last = None
            for ft in range(NFT):
                last = o.matmul(pa[0][:, 2 * ft:2 * ft + 2],
                                p.rku[:, ft * 128:(ft + 1) * 128], p.BH2,
                                start=(ft == 0), stop=(ft == NFT - 1))
            last.then_inc(sem, inc)
        t_dm = S.job("tensor", dj, waits=[t_til[cj], t_fe[3]], inc=1)

        def dej(o, sem, inc):
            o.tensor_copy(out=p.dtm[:, :], in_=pa[0][:, 0:32]).then_inc(sem, inc)
        t_de = S.job("vector", dej, waits=[t_dm], inc=1)

        t_sco = {}
        t_sce = {}
        for grp in range(4):
            def scj(o, sem, inc, grp=grp):
                last = None
                for fl in range(4):
                    ft = grp * 4 + fl
                    o.matmul(psE[:, fl * 128:(fl + 1) * 128],
                             p.ktil[0:64, ft * 128:(ft + 1) * 128],
                             p.rtil[0:64, ft * 128:(ft + 1) * 128],
                             start=True, stop=True, tile_position=(0, 0))
                    last = o.matmul(psO[:, fl * 128:(fl + 1) * 128],
                                    p.ktil[64:128, ft * 128:(ft + 1) * 128],
                                    p.rtil[64:128, ft * 128:(ft + 1) * 128],
                                    start=True, stop=True, tile_position=(64, 0))
                last.then_inc(sem, inc)
            t_sco[grp] = S.job("tensor", scj,
                               waits=[t_til[cj], t_sce.get(grp - 1, ("sync", 0))],
                               inc=1)

            def scej(o, sem, inc, grp=grp):
                for fl in range(4):
                    ft = grp * 4 + fl
                    hE, hO = 2 * ft, 2 * ft + 1
                    o.tensor_mul(p.scT[:, hE * 128:(hE + 1) * 128],
                                 psE[:, fl * 128:(fl + 1) * 128], p.MASKU)
                    inst = o.tensor_mul(p.scT[:, hO * 128:(hO + 1) * 128],
                                        psO[:, fl * 128:(fl + 1) * 128], p.MASKU)
                inst.then_inc(sem, inc)
            t_sce[grp] = S.job("vector", scej, waits=[t_sco[grp]], inc=1)

        def s0j(o, sem, inc, j=j):
            for ft in range(NFT):
                o.tensor_scalar_mul(p.S0A[0:64, ft * 64:(ft + 1) * 64],
                                    p.srun[0:64, ft * 64:(ft + 1) * 64],
                                    p.enm[0:64, ft * 8 + j:ft * 8 + j + 1])
                inst = o.tensor_scalar_mul(
                    p.S0B[64:128, ft * 64:(ft + 1) * 64],
                    p.srun[64:128, ft * 64:(ft + 1) * 64],
                    p.enm[64:128, ft * 8 + j:ft * 8 + j + 1])
            inst.then_inc(sem, inc)
        t_s0 = S.job("vector", s0j, waits=[t_sr_prev, t_ym3_prev], inc=1)

        def srj(o, sem, inc, j=j):
            for ft in range(NFT):
                inst = o.scalar_tensor_tensor(
                    p.srun[:, ft * 64:(ft + 1) * 64],
                    p.srun[:, ft * 64:(ft + 1) * 64],
                    p.A_ch[:, ft * 8 + j:ft * 8 + j + 1],
                    p.sloc[:, j * 1024 + ft * 64:j * 1024 + (ft + 1) * 64],
                    Alu.mult, Alu.add)
            inst.then_inc(sem, inc)
        t_sr_prev = S.job("vector", srj, waits=[t_s0], inc=1)

        t_ym = {}
        for bank in range(4):
            def yj(o, sem, inc, bank=bank, co=co):
                last = None
                for fl in range(4):
                    ft = bank * 4 + fl
                    hE, hO = 2 * ft, 2 * ft + 1
                    o.matmul(py[:, ft * 128:ft * 128 + 64],
                             p.scT[:, hE * 128:(hE + 1) * 128],
                             p.vch[:, co + ft * 128:co + ft * 128 + 64],
                             start=(fl == 0), stop=False)
                    o.matmul(py[:, ft * 128 + 64:(ft + 1) * 128],
                             p.scT[:, hO * 128:(hO + 1) * 128],
                             p.vch[:, co + ft * 128 + 64:co + (ft + 1) * 128],
                             start=False, stop=False)
                    o.matmul(py[:, ft * 128:ft * 128 + 64],
                             p.rtil[:, ft * 128:(ft + 1) * 128],
                             p.S0A[:, ft * 64:(ft + 1) * 64],
                             start=False, stop=False)
                    last = o.matmul(py[:, ft * 128 + 64:(ft + 1) * 128],
                                    p.rtil[:, ft * 128:(ft + 1) * 128],
                                    p.S0B[:, ft * 64:(ft + 1) * 64],
                                    start=False, stop=(fl == 3))
                last.then_inc(sem, inc)
            t_ym[bank] = S.job("tensor", yj,
                               waits=[t_sce[bank], t_s0, t_ye_prev], inc=1)

        def yev(o, sem, inc):
            o.tensor_copy(out=p.ysb[:, :], in_=py[:, :]).then_inc(sem, inc)
        t_yev = S.job("vector", yev, waits=[t_ym[3], t_dbg_prev], inc=1)
        t_dbgy = None
        if p.debug:
            def dbgyraw(o, sem, inc, j=j):
                o.dma_start(out=p.dbg_yraw[:, j * C:(j + 1) * C],
                            in_=p.ysb[:, :]).then_inc(sem, 16)
            t_dbgy = S.job("sync", dbgyraw, waits=[t_yev], inc=16, dma=True)

        def yej(o, sem, inc, co=co):
            for h in range(H):
                inst = o.scalar_tensor_tensor(
                    p.ysb[:, h * 64:(h + 1) * 64],
                    p.vch[:, co + h * 64:co + (h + 1) * 64],
                    p.dtm[:, h:h + 1],
                    p.ysb[:, h * 64:(h + 1) * 64],
                    Alu.mult, Alu.add)
            inst.then_inc(sem, inc)
        t_ye = S.job("vector", yej,
                     waits=[t_yev, t_de] + ([t_dbgy] if t_dbgy else []), inc=1)
        if p.debug:
            def dbgydv(o, sem, inc, j=j):
                o.dma_start(out=p.dbg_ydv[:, j * C:(j + 1) * C],
                            in_=p.ysb[:, :]).then_inc(sem, 16)
            t_dbgydv = S.job("sync", dbgydv, waits=[t_ye], inc=16, dma=True)
        t_ye_prev = t_ye
        t_ym3_prev = t_ym[3]

        def sqj(o, sem, inc):
            o.square(out=p.cmk[:, :], in_=p.ysb[:, :]).then_inc(sem, inc)
        t_sq = S.job("scalar", sqj, waits=[t_ye], inc=1)

        def st1j(o, sem, inc):
            o.reduce_sum(p.gstat[:, 0:32], v3(p.ysb, 64), axis=mybir.AxisListType.X)
            o.reduce_sum(p.gstat[:, 32:64], v3(p.cmk, 64), axis=mybir.AxisListType.X)
            o.tensor_scalar(p.gstat[:, 64:96], p.gstat[:, 0:32],
                            1.0 / 64.0, None, Alu.mult)
            o.tensor_scalar(p.gstat[:, 96:128], p.gstat[:, 32:64],
                            1.0 / 64.0, None, Alu.mult)
            o.tensor_mul(p.gstat[:, 128:160], p.gstat[:, 64:96], p.gstat[:, 64:96])
            o.tensor_sub(p.gstat[:, 0:32], p.gstat[:, 96:128],
                         p.gstat[:, 128:160]).then_inc(sem, inc)
        t_st1 = S.job("vector", st1j, waits=[t_ye, t_sq], inc=1)

        def st2j(o, sem, inc):
            o.activation(out=p.gstat[:, 160:192], in_=p.gstat[:, 0:32],
                         func=Act.Sqrt, bias=p.epsb[:, 0:1]).then_inc(sem, inc)
        t_st2 = S.job("scalar", st2j, waits=[t_st1], inc=1)

        def st3a(o, sem, inc):
            o.reciprocal(p.gstat[:, 192:224],
                         p.gstat[:, 160:192]).then_inc(sem, inc)
        t_s3a = S.job("vector", st3a, waits=[t_st2], inc=1)

        def st3b(o, sem, inc):
            o.copy(out=p.gstat[:, 96:128],
                   in_=p.gstat[:, 192:224]).then_inc(sem, inc)
        t_s3b = S.job("scalar", st3b, waits=[t_s3a], inc=1)

        def st3c(o, sem, inc):
            o.scalar_tensor_tensor(p.gstat[:, 224:256], p.gstat[:, 64:96],
                                   -1.0, p.gstat[:, 96:128],
                                   Alu.mult, Alu.mult).then_inc(sem, inc)
        t_st3 = S.job("vector", st3c, waits=[t_s3b], inc=1)

        def gaj(o, sem, inc):
            for h in range(H):
                inst = o.scalar_tensor_tensor(
                    p.ysb[:, h * 64:(h + 1) * 64],
                    p.ysb[:, h * 64:(h + 1) * 64],
                    p.gstat[:, 96 + h:97 + h],
                    p.gstat[:, 224 + h:225 + h].broadcast_to((128, 64)),
                    Alu.mult, Alu.add)
            inst.then_inc(sem, inc)
        t_ga = S.job("vector", gaj,
                     waits=[t_st3] + ([t_dbgydv] if p.debug else []), inc=1)
        if p.debug:
            def dbggst(o, sem, inc, j=j):
                o.dma_start(out=p.dbg_gst[:, j * 256:(j + 1) * 256],
                            in_=p.gstat[:, :]).then_inc(sem, 16)
            t_dbggst = S.job("sync", dbggst, waits=[t_st3, t_ga], inc=16, dma=True)

        t_yt = {}
        t_z1 = {}
        t_zz = {}
        for grp in range(4):
            def ytj(o, sem, inc, grp=grp):
                last = None
                for fl in range(4):
                    ft = grp * 4 + fl
                    last = o.transpose(pa[grp % 2][:, fl * 128:(fl + 1) * 128],
                                       p.ysb[:, ft * 128:(ft + 1) * 128], p.I128F)
                last.then_inc(sem, inc)
            t_yt[grp] = S.job("tensor", ytj,
                              waits=[t_ga, t_z1.get(grp - 2, ("sync", 0))], inc=1)

            def z1j(o, sem, inc, grp=grp):
                for fl in range(4):
                    ft = grp * 4 + fl
                    inst = o.scalar_tensor_tensor(
                        p.tmpA[:, fl * 128:(fl + 1) * 128],
                        pa[grp % 2][:, fl * 128:(fl + 1) * 128],
                        p.lnw_sb[:, ft:ft + 1],
                        p.lnb_sb[:, ft:ft + 1].broadcast_to((128, 128)),
                        Alu.mult, Alu.add)
                inst.then_inc(sem, inc)
            t_z1[grp] = S.job("vector", z1j,
                              waits=[t_yt[grp], t_zz.get(grp - 1, ("sync", 0))], inc=1)

            def zzj(o, sem, inc, grp=grp, cj=cj):
                o.tensor_mul(
                    v3(p.zsc, SCT)[:, grp * 4:(grp + 1) * 4,
                                   cj * 128:(cj + 1) * 128],
                    v3(p.tmpA[:, 0:512], 128)[:, :, :],
                    v3(p.gsc, SCT)[:, grp * 4:(grp + 1) * 4,
                                   cj * 128:(cj + 1) * 128]).then_inc(sem, inc)
            t_zz[grp] = S.job("vector", zzj, waits=[t_z1[grp], t_g], inc=1)
        t_z_prev = t_zz[3]

        if p.debug:
            def dbgj(o, sem, inc, j=j):
                o.dma_start(out=p.dbg_y[:, j * C:(j + 1) * C],
                            in_=p.ysb[:, :]).then_inc(sem, 16)
                o.dma_start(out=p.dbg_srun[:, j * 1024:(j + 1) * 1024],
                            in_=p.srun[:, :]).then_inc(sem, 16)
            t_dbg1 = S.job("sync", dbgj, waits=[t_ga, t_sr_prev], inc=32,
                           dma=True)

            def dbg2(o, sem, inc, j=j):
                o.dma_start(out=p.dbg_sct[:, j * 4096:(j + 1) * 4096],
                            in_=p.scT[:, :]).then_inc(sem, 16)
                o.dma_start(out=p.dbg_s0[:, j * 2048:j * 2048 + 1024],
                            in_=p.S0A[:, :]).then_inc(sem, 16)
                o.dma_start(out=p.dbg_s0[:, j * 2048 + 1024:(j + 1) * 2048],
                            in_=p.S0B[:, :]).then_inc(sem, 16)
                o.dma_start(out=p.dbg_dtm[:, j * 32:(j + 1) * 32],
                            in_=p.dtm[:, :]).then_inc(sem, 16)
                o.dma_start(out=p.dbg_til[:, j * 4096:j * 4096 + 2048],
                            in_=p.ktil[:, :]).then_inc(sem, 16)
                o.dma_start(out=p.dbg_til[:, j * 4096 + 2048:(j + 1) * 4096],
                            in_=p.rtil[:, :]).then_inc(sem, 16)
            t_dbg2 = S.job("gpsimd", dbg2, waits=[t_sce[3], t_s0, t_de,
                                                  t_til[cj], t_ym[3]],
                           inc=96, dma=True)
            t_dbg_prev = t_dbg1
            t_dbgall_prev = [t_dbg1, t_dbg2, t_dbgydv, t_dbggst, t_dbgy]

    return t_z_prev


def st_wo_ln1(p, sc, t_z, t_gdone=("sync", 0)):
    S = p.s
    po = [pb(p, 4), pb(p, 5)]
    psu = pb(p, 6, rows=1)
    psq = pb(p, 7, rows=1)
    pbc = pb(p, 0, cols=1024)

    t_w = {}
    t_mm = {}
    t_r1 = {}
    t_sq = {}
    t_st = {}
    t_xd = {}
    for mt in range(NFT):
        def wj(o, sem, inc, mt=mt):
            o.dma_start(out=v3(p.wsl[:, (mt % 2) * C:(mt % 2 + 1) * C], 128),
                        in_=p.w_l["o"][mt]).then_inc(sem, inc)
        t_w[mt] = S.job("gpsimd", wj,
                        waits=[t_mm.get(mt - 2, t_gdone)], inc=16, dma=True)

        def xj(o, sem, inc, mt=mt):
            o.dma_start(out=p.xst[:, (mt % 2) * 513:(mt % 2) * 513 + 512],
                        in_=p.xT[mt * 128:(mt + 1) * 128,
                                 sc * SCT + 1:(sc + 1) * SCT + 1]).then_inc(sem, inc)
        t_xd[mt] = S.job("sync", xj, waits=[t_r1.get(mt - 2, ("sync", 0))], inc=16, dma=True)

        def tj(o, sem, inc, mt=mt):
            last = None
            for kt in range(NFT):
                last = o.matmul(
                    po[mt % 2][:, :],
                    p.wsl[:, (mt % 2) * C + kt * 128:(mt % 2) * C + (kt + 1) * 128],
                    p.zsc[:, kt * SCT:(kt + 1) * SCT],
                    start=(kt == 0), stop=(kt == NFT - 1))
            last.then_inc(sem, inc)
        t_mm[mt] = S.job("tensor", tj,
                         waits=[t_w[mt], t_z, t_r1.get(mt - 2, ("sync", 0))], inc=1)

        def rj(o, sem, inc, mt=mt):
            o.tensor_add(p.r1[:, mt * SCT:(mt + 1) * SCT], po[mt % 2][:, :],
                         p.xst[:, (mt % 2) * 513:(mt % 2) * 513 + 512]) \
                .then_inc(sem, inc)
        t_r1[mt] = S.job("vector", rj, waits=[t_mm[mt], t_xd[mt]], inc=1)

        def sqj(o, sem, inc, mt=mt):
            o.square(out=p.r1sq[:, (mt % 2) * 512:(mt % 2 + 1) * 512],
                     in_=p.r1[:, mt * SCT:(mt + 1) * SCT]).then_inc(sem, inc)
        t_sq[mt] = S.job("scalar", sqj,
                         waits=[t_r1[mt], t_st.get(mt - 2, ("sync", 0))], inc=1)

        def stj(o, sem, inc, mt=mt):
            o.matmul(psu[:, :], p.ONES1, p.r1[:, mt * SCT:(mt + 1) * SCT],
                     start=(mt == 0), stop=(mt == NFT - 1))
            o.matmul(psq[:, :], p.ONES1,
                     p.r1sq[:, (mt % 2) * 512:(mt % 2 + 1) * 512],
                     start=(mt == 0), stop=(mt == NFT - 1)).then_inc(sem, inc)
        t_st[mt] = S.job("tensor", stj, waits=[t_sq[mt]], inc=1)

    def rowj(o, sem, inc):
        o.tensor_scalar(p.rows[:, 0:512], psu[:, :], 1.0 / C, None, Alu.mult)
        o.tensor_scalar(p.rows[:, 512:1024], psq[:, :], 1.0 / C, None, Alu.mult)
        o.tensor_mul(p.rows[:, 1024:1536], p.rows[:, 0:512], p.rows[:, 0:512])
        o.tensor_sub(p.rows[:, 512:1024], p.rows[:, 512:1024],
                     p.rows[:, 1024:1536]).then_inc(sem, inc)
    t_row = S.job("vector", rowj, waits=[t_st[NFT - 1]], inc=1)

    def rsj(o, sem, inc):
        o.activation(out=p.rows[:, 1024:1536], in_=p.rows[:, 512:1024],
                     func=Act.Sqrt, bias=p.epsb[0:1, 1:2]).then_inc(sem, inc)
    t_rs = S.job("scalar", rsj, waits=[t_row], inc=1)

    def rcj(o, sem, inc):
        o.reciprocal(p.rows[:, 512:1024], p.rows[:, 1024:1536]).then_inc(sem, inc)
    t_rc = S.job("vector", rcj, waits=[t_rs], inc=1)

    def bcj(o, sem, inc):
        o.matmul(pbc[:, 0:512], p.ONESR, p.rows[:, 0:512], start=True, stop=True)
        o.matmul(pbc[:, 512:1024], p.ONESR, p.rows[:, 512:1024],
                 start=True, stop=True).then_inc(sem, inc)
    t_bc = S.job("tensor", bcj, waits=[t_rc], inc=1)

    t_x1 = {}
    t_x1d = {}
    for mt in range(NFT):
        def aj(o, sem, inc, mt=mt):
            o.tensor_sub(p.r1[:, mt * SCT:(mt + 1) * SCT],
                         p.r1[:, mt * SCT:(mt + 1) * SCT], pbc[:, 0:512])
            o.tensor_mul(p.r1[:, mt * SCT:(mt + 1) * SCT],
                         p.r1[:, mt * SCT:(mt + 1) * SCT],
                         pbc[:, 512:1024]).then_inc(sem, inc)
        t_x1[mt] = S.job("vector", aj, waits=[t_bc], inc=1)

        def dj(o, sem, inc, mt=mt):
            o.dma_start(out=p.spill_x1[:, mt * TC + sc * SCT:mt * TC + (sc + 1) * SCT],
                        in_=p.r1[:, mt * SCT:(mt + 1) * SCT]).then_inc(sem, inc)
        t_x1d[mt] = S.job("sync", dj, waits=[t_x1[mt]], inc=16, dma=True)
    return t_x1d[NFT - 1]


def ph_pass2_sc(p, sc):
    S = p.s
    xdone = st_xload(p, sc)
    t_m = st_mstage(p, [3, 4], xdone)
    t_mixr, t_ta_r = st_mix(p, 3, p.mixB, t_m[3], 0)

    def evr(o, mt, ps, sem, inc):
        o.tensor_copy(out=p.ksc[:, mt * SCT:(mt + 1) * SCT],
                      in_=ps).then_inc(sem, inc)
    t_re = st_proj_fm(p, "r", p.mixB, t_mixr, evr,
                      fence=[t_m[3], t_m[4]])
    t_mixg, t_ta_g = st_mix(p, 4, p.mixA, t_m[4], 1, fence=[t_ta_r])

    def evg(o, mt, ps, sem, inc):
        o.activation(out=p.gsc[:, mt * SCT:(mt + 1) * SCT], in_=ps,
                     func=Act.Silu).then_inc(sem, inc)
    t_ge = st_proj_fm(p, "g", p.mixA, t_mixg, evg, eng_ev="scalar",
                      fence=[t_re[NFT - 1]])
    t_z = st_scan(p, sc, t_re[NFT - 1], t_ge[NFT - 1])
    t_x1 = st_wo_ln1(p, sc, t_z, t_gdone=t_ge[NFT - 1])
    if p.debug:
        def dbgz(o, sem, inc):
            for ft in range(NFT):
                o.dma_start(
                    out=p.dbg_z[:, ft * TC + sc * SCT:ft * TC + (sc + 1) * SCT],
                    in_=p.zsc[:, ft * SCT:(ft + 1) * SCT]).then_inc(sem, 16)
        p.s.job("gpsimd", dbgz, waits=[t_z], inc=16 * NFT, dma=True)
    return t_x1


def ph_ffn_sc(p, sc, t_qag):
    S = p.s
    pgu = [pb(p, 0), pb(p, 1)]
    pss = pb(p, 2, rows=1)

    t_xc = {}
    for kt in range(NFT):
        def xj(o, sem, inc, kt=kt):
            o.dma_start(out=p.f_x1t[:, (kt % 2) * SCT:(kt % 2 + 1) * SCT],
                        in_=p.spill_x1[:, kt * TC + sc * SCT:kt * TC + (sc + 1) * SCT]) \
                .then_inc(sem, inc)
        t_xd = S.job("sync", xj, waits=[t_xc.get(kt - 2, ("sync", 0))], inc=16, dma=True)

        def cjb(o, sem, inc, kt=kt):
            o.tensor_copy(out=p.f_x1b[:, kt * SCT:(kt + 1) * SCT],
                          in_=p.f_x1t[:, (kt % 2) * SCT:(kt % 2 + 1) * SCT]) \
                .then_inc(sem, inc)
        t_xc[kt] = S.job("vector", cjb, waits=[t_xd], inc=1)
    t_x1b = t_xc[NFT - 1]

    order = [x for i in range(64) for x in (i, 64 + i)]
    t_w = {}
    t_mm = {}
    t_hv = {}
    t_hs = {}
    t_ss = {}
    for idx, mt in enumerate(order):
        def wj(o, sem, inc, mt=mt, idx=idx):
            o.dma_start(
                out=v3(p.f_gusl[:, (idx % 2) * C:(idx % 2 + 1) * C], 128),
                in_=p.qgu_all[mt // 16, mt % 16]).then_inc(sem, inc)
        t_w[idx] = S.job("gpsimd", wj,
                         waits=[t_qag, t_mm.get(idx - 2, ("sync", 0))], inc=16,
                         dma=True)

        def tj(o, sem, inc, mt=mt, idx=idx):
            last = None
            for kt in range(NFT):
                last = o.matmul(
                    pgu[idx % 2][:, :],
                    p.f_gusl[:, (idx % 2) * C + kt * 128:(idx % 2) * C + (kt + 1) * 128],
                    p.f_x1b[:, kt * SCT:(kt + 1) * SCT],
                    start=(kt == 0), stop=(kt == NFT - 1))
            last.then_inc(sem, inc)
        gate_pair = t_hv.get(idx - 2, ("sync", 0))
        t_mm[idx] = S.job("tensor", tj,
                          waits=[t_w[idx], t_x1b,
                                 t_hs.get(idx - 2, ("sync", 0)), gate_pair], inc=1)

        if idx % 2 == 0:
            def gj(o, sem, inc, idx=idx):
                o.activation(out=p.f_sg[:, :], in_=pgu[idx % 2][:, :],
                             func=Act.Silu).then_inc(sem, inc)
            t_hs[idx] = S.job("scalar", gj, waits=[t_mm[idx]], inc=1)
        else:
            i = idx // 2

            def hj(o, sem, inc, i=i, idx=idx):
                o.tensor_mul(p.f_h[:, i * SCT:(i + 1) * SCT], pgu[idx % 2][:, :],
                             p.f_sg[:, :]).then_inc(sem, inc)
            t_hv[idx] = S.job("vector", hj, waits=[t_mm[idx], t_hs[idx - 1]], inc=1)

            def h2j(o, sem, inc, i=i):
                o.square(out=p.f_h2[:, (i % 2) * SCT:(i % 2 + 1) * SCT],
                         in_=p.f_h[:, i * SCT:(i + 1) * SCT]).then_inc(sem, inc)
            t_hs[idx] = S.job("scalar", h2j,
                              waits=[t_hv[idx], t_ss.get(idx - 2, ("sync", 0))],
                              inc=1)

            def ssj(o, sem, inc, i=i):
                o.matmul(pss[:, :], p.ONES1,
                         p.f_h2[:, (i % 2) * SCT:(i % 2 + 1) * SCT],
                         start=(i == 0), stop=(i == 63)).then_inc(sem, inc)
            t_ss[idx] = S.job("tensor", ssj, waits=[t_hs[idx]], inc=1)

    def rmj(o, sem, inc):
        o.tensor_scalar(p.f_rows[:, 0:512], pss[:, :], 1.0 / 8192.0,
                        None, Alu.mult).then_inc(sem, inc)
    t_rm = S.job("vector", rmj, waits=[t_ss[127]], inc=1)

    def rsj(o, sem, inc):
        o.activation(out=p.f_rows[:, 1024:1536], in_=p.f_rows[:, 0:512],
                     func=Act.Sqrt, bias=p.epsb[0:1, 2:3]).then_inc(sem, inc)
    t_rs = S.job("scalar", rsj, waits=[t_rm], inc=1)

    def rrj(o, sem, inc):
        o.reciprocal(p.f_rows[:, 512:1024], p.f_rows[:, 1024:1536]) \
            .then_inc(sem, inc)
    t_rr = S.job("vector", rrj, waits=[t_rs], inc=1)

    pbc = pb(p, 3)

    def bcj(o, sem, inc):
        o.matmul(pbc[:, :], p.ONESR, p.f_rows[:, 512:1024],
                 start=True, stop=True).then_inc(sem, inc)
    t_bc = S.job("tensor", bcj, waits=[t_rr], inc=1)

    def bcv(o, sem, inc):
        o.tensor_copy(out=p.f_rstdb[:, :], in_=pbc[:, :]).then_inc(sem, inc)
    t_bcv = S.job("vector", bcv, waits=[t_bc], inc=1)

    ppr = [pb(p, 4), pb(p, 5)]
    psu2 = pb(p, 6, rows=1)
    psq2 = pb(p, 7, rows=1)
    t_pw = {}
    t_pm = {}
    t_r2 = {}
    t_q2 = {}
    t_s2 = {}
    t_xt = {}
    for mt in range(NFT):
        for half in range(2):
            g = mt * 2 + half

            def wj(o, sem, inc, mt=mt, half=half, g=g):
                o.dma_start(
                    out=v3(p.f_prsl[:, (g % 2) * 4096:(g % 2 + 1) * 4096], 128),
                    in_=p.qpr_all[mt // 2, mt % 2][:, half * 32:(half + 1) * 32, :]) \
                    .then_inc(sem, inc)
            t_pw[g] = S.job("gpsimd", wj,
                            waits=[t_qag, t_pm.get((g - 2) // 2, ("sync", 0))
                                   if g >= 2 else ("sync", 0)], inc=16, dma=True)

        def xj(o, sem, inc, mt=mt):
            o.dma_start(out=p.f_x1t[:, (mt % 2) * SCT:(mt % 2 + 1) * SCT],
                        in_=p.spill_x1[:, mt * TC + sc * SCT:mt * TC + (sc + 1) * SCT]) \
                .then_inc(sem, inc)
        t_xt[mt] = S.job("sync", xj,
                         waits=[t_x1b, t_r2.get(mt - 2, ("sync", 0))], inc=16,
                         dma=True)

        def tj(o, sem, inc, mt=mt):
            last = None
            for kt in range(64):
                g = mt * 2 + kt // 32
                b = (g % 2) * 4096 + (kt % 32) * 128
                last = o.matmul(
                    ppr[mt % 2][:, :], p.f_prsl[:, b:b + 128],
                    p.f_h[:, kt * SCT:(kt + 1) * SCT],
                    start=(kt == 0), stop=(kt == 63))
            last.then_inc(sem, inc)
        t_pm[mt] = S.job("tensor", tj,
                         waits=[t_pw[mt * 2], t_pw[mt * 2 + 1],
                                t_hv.get(127, ("sync", 0)),
                                t_r2.get(mt - 2, ("sync", 0))], inc=1)

        def rj(o, sem, inc, mt=mt):
            o.tensor_mul(p.f_r2[:, mt * SCT:(mt + 1) * SCT], ppr[mt % 2][:, :],
                         p.f_rstdb[:, :])
            o.tensor_add(p.f_r2[:, mt * SCT:(mt + 1) * SCT],
                         p.f_r2[:, mt * SCT:(mt + 1) * SCT],
                         p.f_x1t[:, (mt % 2) * SCT:(mt % 2 + 1) * SCT]) \
                .then_inc(sem, inc)
        t_r2[mt] = S.job("vector", rj, waits=[t_pm[mt], t_xt[mt], t_bcv], inc=1)

        def qj(o, sem, inc, mt=mt):
            o.square(out=p.f_h2[:, (mt % 2) * SCT:(mt % 2 + 1) * SCT],
                     in_=p.f_r2[:, mt * SCT:(mt + 1) * SCT]).then_inc(sem, inc)
        t_q2[mt] = S.job("scalar", qj,
                         waits=[t_r2[mt], t_s2.get(mt - 2, ("sync", 0))], inc=1)

        def sj(o, sem, inc, mt=mt):
            o.matmul(psu2[:, :], p.ONES1, p.f_r2[:, mt * SCT:(mt + 1) * SCT],
                     start=(mt == 0), stop=(mt == NFT - 1))
            o.matmul(psq2[:, :], p.ONES1,
                     p.f_h2[:, (mt % 2) * SCT:(mt % 2 + 1) * SCT],
                     start=(mt == 0), stop=(mt == NFT - 1)).then_inc(sem, inc)
        t_s2[mt] = S.job("tensor", sj, waits=[t_q2[mt]], inc=1)

    def rowj(o, sem, inc):
        o.tensor_scalar(p.f_rows[:, 0:512], psu2[:, :], 1.0 / C, None, Alu.mult)
        o.tensor_scalar(p.f_rows[:, 512:1024], psq2[:, :], 1.0 / C, None, Alu.mult)
        o.tensor_mul(p.f_rows[:, 1024:1536], p.f_rows[:, 0:512], p.f_rows[:, 0:512])
        o.tensor_sub(p.f_rows[:, 512:1024], p.f_rows[:, 512:1024],
                     p.f_rows[:, 1024:1536]).then_inc(sem, inc)
    t_row = S.job("vector", rowj, waits=[t_s2[NFT - 1]], inc=1)

    def lsj(o, sem, inc):
        o.activation(out=p.f_rows[:, 1024:1536], in_=p.f_rows[:, 512:1024],
                     func=Act.Sqrt, bias=p.epsb[0:1, 1:2]).then_inc(sem, inc)
    t_ls = S.job("scalar", lsj, waits=[t_row], inc=1)

    def lrj(o, sem, inc):
        o.reciprocal(p.f_rows[:, 512:1024], p.f_rows[:, 1024:1536]) \
            .then_inc(sem, inc)
    t_lr = S.job("vector", lrj, waits=[t_ls], inc=1)

    pbc2 = pb(p, 2, cols=1024)

    def bc2j(o, sem, inc):
        o.matmul(pbc2[:, 0:512], p.ONESR, p.f_rows[:, 0:512], start=True, stop=True)
        o.matmul(pbc2[:, 512:1024], p.ONESR, p.f_rows[:, 512:1024],
                 start=True, stop=True).then_inc(sem, inc)
    t_bc2 = S.job("tensor", bc2j, waits=[t_lr], inc=1)

    t_o = {}
    for mt in range(NFT):
        def aj(o, sem, inc, mt=mt):
            o.tensor_sub(p.f_r2[:, mt * SCT:(mt + 1) * SCT],
                         p.f_r2[:, mt * SCT:(mt + 1) * SCT], pbc2[:, 0:512])
            o.tensor_mul(p.f_r2[:, mt * SCT:(mt + 1) * SCT],
                         p.f_r2[:, mt * SCT:(mt + 1) * SCT],
                         pbc2[:, 512:1024]).then_inc(sem, inc)
        t_a = S.job("vector", aj, waits=[t_bc2], inc=1)

        def oj(o, sem, inc, mt=mt):
            o.dma_start(out=p.out[mt * 128:(mt + 1) * 128, sc * SCT:(sc + 1) * SCT],
                        in_=p.f_r2[:, mt * SCT:(mt + 1) * SCT]).then_inc(sem, inc)
        t_o[mt] = S.job("sync", oj, waits=[t_a], inc=16, dma=True)
    return t_o[NFT - 1]


def plan(p, upto="full"):
    S = p.s
    ph_consts(p)
    if upto == "consts":
        return
    t_qag = None
    if upto in ("full", "ffn", "quant"):
        ph_quant(p)
        t_qag = p.t_qag
    if upto == "quant":
        if p.debug:
            def qdump(o, sem, inc):
                cid = 0
                for mtl in range(16):
                    for r8 in range(8):
                        o.dma_start(
                            out=p.dbg_qgu[mtl, :, r8 * 2:(r8 + 1) * 2, :],
                            in_=p.qgu_mine[mtl, :, r8 * 2:(r8 + 1) * 2, :]) \
                            .then_inc(sem, 16)
                for mtl in range(2):
                    for r8 in range(8):
                        o.dma_start(
                            out=p.dbg_qpr[mtl, :, r8 * 8:(r8 + 1) * 8, :],
                            in_=p.qpr_mine[mtl, :, r8 * 8:(r8 + 1) * 8, :]) \
                            .then_inc(sem, 16)
            S.job("gpsimd", qdump, waits=[t_qag], inc=16 * (128 + 16), dma=True)
        return
    p.pcl_ap = pb(p, 7, cols=384)
    for sc in range(2):
        ph_pass1_sc(p, sc, p.pcl_ap)
        S.barrier(skip_cc=True)
    if upto == "pass1":
        if p.debug:
            def dbgs(o, sem, inc):
                o.dma_start(out=p.dbg_sloc[:, :], in_=p.sloc[:, :]).then_inc(sem, 16)
            S.job("gpsimd", dbgs, inc=16, dma=True)
        return
    ph_msg(p)
    if upto == "msg":
        return
    ph_combine(p)
    if p.debug:
        def dbgs(o, sem, inc):
            o.dma_start(out=p.dbg_sloc[:, :], in_=p.sloc[:, :]).then_inc(sem, 16)
        S.job("gpsimd", dbgs, inc=16, dma=True)
    if upto == "combine":
        return
    for sc in range(2):
        ph_pass2_sc(p, sc)
        S.barrier(skip_cc=True)
    if upto == "pass2":
        return
    for sc in range(2):
        ph_ffn_sc(p, sc, t_qag)
        S.barrier(skip_cc=True)


def finish(p):
    p.s.barrier()


def make_nc(debug=False, upto="full", no_cc=False):
    p = build(debug=debug)
    if upto == "msg" or no_cc:
        p.no_cc = True
    plan(p, upto=upto)
    finish(p)
    with p.nc.Block() as block:
        p.s.emit(block)
    return p


def host_inputs(inputs):
    x = np.ascontiguousarray(np.asarray(inputs["L0_attn_x"], np.float32)[0].T)

    def w2l(w):
        return np.ascontiguousarray(
            np.asarray(w, np.float32).reshape(16, 128, 16, 128).transpose(2, 1, 0, 3))
    shared = {
        "wk_l": w2l(inputs["L0_attn_wk"]), "wr_l": w2l(inputs["L0_attn_wr"]),
        "wg_l": w2l(inputs["L0_attn_wg"]), "wo_l": w2l(inputs["L0_attn_wo"]),
        "wv_r": np.ascontiguousarray(np.asarray(inputs["L0_attn_wv"], np.float32)),
        "w1_l": np.ascontiguousarray(
            np.asarray(inputs["L0_attn_w1"], np.float32).reshape(16, 128, 160)),
        "w2_l": np.ascontiguousarray(np.asarray(inputs["L0_attn_w2"], np.float32)),
        "tdw1_l": np.ascontiguousarray(
            np.asarray(inputs["L0_attn_td_w1"], np.float32).reshape(16, 128, 64)),
        "tdw2_r": np.ascontiguousarray(np.asarray(inputs["L0_attn_td_w2"], np.float32)),
        "tdb": np.ascontiguousarray(np.broadcast_to(
            np.asarray(inputs["L0_attn_time_decay"], np.float32).reshape(1, C),
            (128, C))),
        "maa6": np.ascontiguousarray(np.stack(
            [np.asarray(inputs[f"L0_attn_time_maa_{n}"], np.float32).reshape(C)
             for n in "xwkvrg"], axis=1).reshape(16, 128, 6).transpose(1, 0, 2)),
        "u_h": np.ascontiguousarray(
            np.asarray(inputs["L0_attn_time_faaaa"], np.float32).reshape(16, 128).T),
        "lnw_h": np.ascontiguousarray(
            np.asarray(inputs["L0_attn_ln_w"], np.float32).reshape(16, 128).T),
        "lnb_h": np.ascontiguousarray(
            np.asarray(inputs["L0_attn_ln_b"], np.float32).reshape(16, 128).T),
    }
    wguT = np.ascontiguousarray(np.asarray(inputs["L0_ffn_w_gate_up"], np.float32).T)
    wprT = np.ascontiguousarray(np.asarray(inputs["L0_ffn_w_proj"], np.float32).T)
    in_maps = []
    for c in range(NCORES):
        prev = x[:, c * TC - 1:c * TC] if c > 0 else np.zeros((C, 1), np.float32)
        xTc = np.ascontiguousarray(
            np.concatenate([prev, x[:, c * TC:(c + 1) * TC]], axis=1))
        pmv = np.zeros((128, 8), np.float32)
        pmv[:, :c] = 1.0
        m = dict(shared)
        m["xT"] = xTc
        m["pm"] = pmv
        m["pmi"] = np.ascontiguousarray(1.0 - pmv)
        m["qgu_f32"] = np.ascontiguousarray(wguT[:, c * C:(c + 1) * C])
        m["qpr_f32"] = np.ascontiguousarray(wprT[:, c * 256:(c + 1) * 256])
        in_maps.append(m)
    return in_maps


def kernel(**inputs):
    in_maps = host_inputs(inputs)
    p = make_nc(debug=False, upto="full")
    res = run_bass_kernel_spmd(p.nc, in_maps, core_ids=list(range(NCORES)))
    outs = [r["out"] for r in res.results]
    return np.concatenate([o.T for o in outs], axis=0)[None]
